# version 07
# baseline (speedup 1.0000x reference)
import math
from contextlib import ExitStack

import numpy as np
import ml_dtypes

import concourse.bass as bass
import concourse.mybir as mybir
from concourse.bass_utils import run_bass_kernel_spmd

F32 = mybir.dt.float32
BF16 = mybir.dt.bfloat16
AF = mybir.ActivationFunctionType
ALU = mybir.AluOpType
AX = mybir.AxisListType
BF = ml_dtypes.bfloat16

PROBE = None

C = 8
D = 2048
KW = 512
LCOMP = 128
NSFX = 8
ISQ = float(1.0 / math.sqrt(128.0))


class Cfg:
    def __init__(self, NT=64, debug=()):
        self.NT = NT
        self.S = NT // C
        self.HT = NT // 2
        self.T = 128 * NT
        self.debug = set(debug)


def own_row_index(cfg, core):
    idx = []
    for t in range(cfg.S):
        g = C * t + core
        idx.extend(range(128 * g, 128 * g + 128))
    return np.array(idx)


def host_consts(cfg, core):
    T = cfg.T
    pos = np.arange(T, dtype=np.float64)
    inv = 10000.0 ** (-np.arange(64, dtype=np.float64) / 64.0)
    ang = pos[:, None] * inv[None, :]
    cosI = np.repeat(np.cos(ang), 2, axis=1).astype(np.float32)
    sinI = np.repeat(np.sin(ang), 2, axis=1).astype(np.float32)
    sinI[:, 0::2] *= -1.0
    invf = 1.0 / (1000000.0 ** (np.arange(0, 64, 2, dtype=np.float64) / 64.0))
    angf = pos[:, None] * invf[None, :]
    cosF = np.concatenate([np.cos(angf), np.cos(angf)], axis=1).astype(np.float32)
    sinF = np.concatenate([np.sin(angf), np.sin(angf)], axis=1).astype(np.float32)
    sinF[:, 0:32] *= -1.0
    own = own_row_index(cfg, core)
    kk = np.arange(128)
    tri = (kk[:, None] <= kk[None, :]).astype(np.float32)
    m = np.zeros((128, NSFX, 8, 128), np.float32)
    for j in range(NSFX):
        if j < core:
            m[:, j] = 1.0
        elif j == core:
            m[:, j] = tri[:, None, :]
    return dict(cosI=cosI, sinI=sinI, cosF=cosF, sinF=sinF,
                cosIq=np.ascontiguousarray(cosI[own]), sinIq=np.ascontiguousarray(sinI[own]),
                cosFq=np.ascontiguousarray(cosF[own]), sinFq=np.ascontiguousarray(sinF[own]),
                masks=m.reshape(128, NSFX, 1024).astype(BF),
                ident=np.eye(128, dtype=BF))


class Sem:
    def __init__(self, es, nc, name):
        self.h = es.enter_context(nc.semaphore(name))
        self.n = 0

    def inc(self, instr, k=1):
        instr.then_inc(self.h, k)
        self.n += k
        return instr

    def wait(self, eng, val=None):
        eng.wait_ge(self.h, self.n if val is None else val)


_EMITTING_BLOCKLIST = {"wait_ge", "sem_inc", "wait_op", "maybe_wait_ge", "Fori",
                       "register", "register64", "reg_mov", "reg_add", "reg_save"}


class EngWrap:

    def __init__(self, es, nc, raw, name):
        self.raw = raw
        self.own = Sem(es, nc, name)
        self.pending = None
        self.pending_claim = None

    def __getattr__(self, name):
        fn = getattr(self.raw, name)
        if name in _EMITTING_BLOCKLIST or not callable(fn):
            return fn

        def wrapped(*a, **kw):
            self.wait()
            instr = fn(*a, **kw)
            self.pending = instr
            return instr

        return wrapped

    def claim(self, sem, k=1):
        assert self.pending is not None
        sem.inc(self.pending, k)
        self.pending_claim = (sem.h, sem.n)
        return self.pending

    def wait(self):
        if self.pending is not None:
            if self.pending_claim is None:
                self.own.inc(self.pending)
                mark = (self.own.h, self.own.n)
            else:
                mark = self.pending_claim
            self.raw.wait_ge(mark[0], mark[1])
            self.pending = None
            self.pending_claim = None


class Chain:

    def __init__(self, wrap):
        self.wrap = wrap

    def put(self, instr):
        return instr

    def pute(self, instr, sem, k=1):
        return self.wrap.claim(sem, k)

    def wait(self):
        self.wrap.wait()


class Arena:

    def __init__(self, nc, es, kb=198):
        self.nbytes = kb * 1024
        self.t = es.enter_context(nc.sbuf_tensor("ARENA", [128, self.nbytes // 2], BF16))
        self.free_list = [(0, self.nbytes)]

    def alloc(self, shape, dt, hi=False):
        per = int(np.prod(shape[1:])) * (2 if dt == BF16 else 4)
        per = (per + 63) & ~63
        if hi:
            for i in range(len(self.free_list) - 1, -1, -1):
                off, size = self.free_list[i]
                if size >= per:
                    self.free_list[i] = (off, size - per)
                    if self.free_list[i][1] == 0:
                        self.free_list.pop(i)
                    return ArenaView(self, off + size - per, per, shape, dt)
        for i, (off, size) in enumerate(self.free_list):
            if size >= per:
                self.free_list[i] = (off + per, size - per)
                if self.free_list[i][1] == 0:
                    self.free_list.pop(i)
                return ArenaView(self, off, per, shape, dt)
        raise MemoryError(f"arena full: want {per}B, free={self.free_list}")

    def free(self, view):
        self.free_list.append((view.off, view.per))
        self.free_list.sort()
        merged = []
        for off, size in self.free_list:
            if merged and merged[-1][0] + merged[-1][1] == off:
                merged[-1] = (merged[-1][0], merged[-1][1] + size)
            else:
                merged.append((off, size))
        self.free_list = [tuple(x) for x in merged]


class ArenaView:
    def __init__(self, arena, off, per, shape, dt):
        self.arena, self.off, self.per, self.shape, self.dt = arena, off, per, shape, dt
        n = int(np.prod(shape[1:]))
        ap = arena.t[:, off // 2: off // 2 + n * (1 if dt == BF16 else 2)]
        if dt != BF16:
            ap = ap.bitcast(dt)
        if len(shape) > 2:
            dims = " ".join(f"d{i}" for i in range(1, len(shape)))
            kw = {f"d{i}": shape[i] for i in range(1, len(shape))}
            ap = ap.rearrange(f"p ({dims}) -> p {dims}", **kw)
        self.ap = ap

    def __getitem__(self, key):
        return self.ap[key]


def build_program(cfg):
    nc = bass.Bass("TRN2", target_bir_lowering=False)
    NT, S, HT = cfg.NT, cfg.S, cfg.HT
    T = cfg.T

    def din(name, shape, dt=F32):
        return nc.dram_tensor(name, shape, dt, kind="ExternalInput").ap()

    x_full = din("x_full", [T, D])
    x_own = din("x_own", [128 * S, D])
    cosI_d = din("cosI", [T, 128]); sinI_d = din("sinI", [T, 128])
    cosF_d = din("cosF", [T, 64]); sinF_d = din("sinF", [T, 64])
    cosIq_d = din("cosIq", [128 * S, 128]); sinIq_d = din("sinIq", [128 * S, 128])
    cosFq_d = din("cosFq", [128 * S, 64]); sinFq_d = din("sinFq", [128 * S, 64])
    masks_d = din("masks", [128, NSFX, 1024], BF16)
    ident_d = din("ident", [128, 128], BF16)
    W_latent_d = din("W_latent", [D, LCOMP]); b_latent_d = din("b_latent", [LCOMP])
    W_d_d = din("W_d", [D, D]); b_d_d = din("b_d", [D])
    W_proj_d = din("W_proj", [D, D]); b_proj_d = din("b_proj", [D])
    ln1_d = din("ln1_w", [D]); ln2_d = din("ln2_w", [D])
    wq_d = din("wq", [D, D]); wk_d = din("wk", [D, KW]); wv_d = din("wv", [D, KW])
    wo_d = din("wo", [D, D])
    qnorm_d = din("qnorm_w", [D]); knorm_d = din("knorm_w", [KW])
    router_w_d = din("router_w", [D, 8]); router_b_d = din("router_bias", [8])
    wg_d = din("wg", [8, D, 1024]); wu_d = din("wu", [8, D, 1024]); wd_d = din("wd", [8, 1024, D])

    out_d = nc.dram_tensor("out_own", [128 * S, D], F32, kind="ExternalOutput").ap()
    dbg_aps = {}

    es = ExitStack()
    AR = Arena(nc, es, kb=202)

    def sbuf(name, shape, dt):
        return es.enter_context(nc.sbuf_tensor(name, shape, dt))

    def sem(name):
        return Sem(es, nc, name)

    ident = sbuf("ident_s", [128, 128], BF16)
    ones_row = sbuf("ones_row", [1, 128], BF16)
    ones_col = sbuf("ones_col", [128, 1], BF16)
    ln1_sb = sbuf("ln1_sb", [128, 16], F32)
    brow_lat = sbuf("brow_lat", [1, LCOMP], BF16)
    biasB = sbuf("biasB", [128, 4], F32)
    stat = sbuf("stat", [128, 2, 16], F32)
    qssh = sbuf("qssh", [128, 2, 16], F32)
    kssh = sbuf("kssh", [128, 8], F32)
    ktmp4 = sbuf("ktmp4", [128, 4], F32)
    qtmp16 = sbuf("qtmp16", [128, 16], F32)
    brow_scr = sbuf("brow_scr", [1, 32], F32)
    lt_scr = sbuf("lt_scr", [128, 16], F32)
    rl_scr = sbuf("rl_scr", [128, 16], F32)
    comb = sbuf("comb", [128, S, 8], F32)
    rbias_bc = sbuf("rbias_bc", [128, 8], F32)
    kvsc = sbuf("kvsc", [128, 2, 128], BF16)
    r2s = sbuf("r2s", [128, S], F32)

    xnT_scr = AR.alloc([128, 16, 128], BF16)
    otT_scr = AR.alloc([128, 16, 128], BF16)
    ot_stage = AR.alloc([128, 2048], BF16)

    PSA = es.enter_context(nc.psum_tensor("PSA", [128, 2, 1024], F32))
    PSB = es.enter_context(nc.psum_tensor("PSB", [128, 1024], F32))
    PSC = es.enter_context(nc.psum_tensor("PSC", [128, 1024], F32))
    PSA_flat = PSA[:, :, :].rearrange("p a b -> p (a b)")

    names = ("s_c0 s_x s_ss s_hop s_hop2 s_xn s_Ta s_TevA s_pj s_pe s_knb s_Tb "
             "s_TevB s_w s_wf s_qk s_exp s_mk s_pv s_sp s_stg s_dnc s_gen s_out "
             "s_dbg s_kssv s_kvmax s_qb s_x2 s_r2 s_sg s_cg "
             "s_x0 s_x1 s_stg0 s_stg1 s_w0 s_w1 s_wgA s_wgB s_wgD s_kvd0 s_kvd1 "
             "s_out0 s_out1 s_ks0 s_ks1 s_vs0 s_vs1 s_ko s_vo s_stg2 s_stg3 s_wA1 s_wB1").split()
    SM = {n: sem(n) for n in names}
    g = lambda n: SM[n]
    V = EngWrap(es, nc, nc.vector, "vch")
    SC = EngWrap(es, nc, nc.scalar, "sch")
    vch, sch = Chain(V), Chain(SC)
    TE, SY, GP = nc.tensor, nc.sync, nc.gpsimd

    from contextlib import contextmanager

    @contextmanager
    def reg_reclaim():
        n0 = nc.n_instructions()
        yield
        from concourse.bass_primitives_rust import RegisterHandle
        emap = {"PE": mybir.EngineType.PE, "Pool": mybir.EngineType.Pool,
                "Activation": mybir.EngineType.Activation,
                "SP": mybir.EngineType.SP, "DVE": mybir.EngineType.DVE}
        seen = set()
        for nm in nc.inst_names()[n0:]:
            ii = nc.get_inst_by_name(nm)
            for arg in list(ii.ins) + list(ii.outs):
                r = getattr(arg, "regref", None) or getattr(arg, "name", None)
                if isinstance(r, str) and ("_tmp_" in r or "_fori_" in r):
                    seen.add(r)
        for r in seen:
            eng = emap.get(r.split("_", 1)[0])
            if eng is None:
                continue
            try:
                nc.free_register(RegisterHandle(name=r, engine=eng))
            except ValueError:
                pass

    def dump(name, src_ap, shape, dt=F32):
        if name not in cfg.debug:
            return
        dd = nc.dram_tensor("dbg_" + name, shape, dt, kind="ExternalOutput").ap()
        dbg_aps[name] = dd
        g("s_dbg").inc(SY.dma_start(out=dd, in_=src_ap), 16)
        g("s_dbg").wait(SY)

    s_c0 = g("s_c0")
    s_c0.inc(SY.dma_start(out=ident[:], in_=ident_d), 16)
    s_c0.inc(SY.dma_start(out=ln1_sb[:], in_=ln1_d.rearrange("(c p) -> p c", p=128), allow_slow_non_contiguous=True), 16)
    g("s_cg").inc(GP.dma_start(out=brow_lat[:], in_=b_latent_d.unsqueeze(0)), 16)
    s_c0.inc(SY.dma_start(out=rbias_bc[:], in_=router_b_d.unsqueeze(0).partition_broadcast(128)), 16)
    V.memset(ones_row[:], 1.0)
    V.memset(ones_col[:], 1.0)
    V.memset(qssh[:].rearrange("p a b -> p (a b)"), 0.0)
    V.memset(kssh[:], 0.0)
    s_c0.wait(V)
    g("s_cg").wait(V)
    nc.all_engine_barrier()

    def emit_ln(par, src_ap, want_rf1, xn_dst=None, junkL=None):
        st = lambda c: stat[:, par, c:c + 1]
        vch.put(V.reduce_sum(st(0), src_ap, axis=AX.X))
        sch.pute(SC.activation(junkL[:], src_ap, AF.Square, accum_out=st(1)), g("s_ss"))
        g("s_ss").wait(V)
        vch.wait()
        vch.put(V.tensor_scalar(st(2), st(0), 1.0 / D, None, ALU.mult))
        vch.put(V.tensor_scalar(st(3), st(1), 1.0 / D, None, ALU.mult))
        vch.wait()
        vch.put(V.scalar_tensor_tensor(st(4), st(2), st(2), st(3), ALU.mult, ALU.subtract))
        vch.wait()
        vch.put(V.tensor_scalar(st(4), st(4), -1.0, 1e-5, ALU.mult, ALU.add))
        vch.wait()
        vch.put(V.reciprocal(st(5), st(4)))
        vch.wait()
        if want_rf1:
            vch.put(V.tensor_scalar(st(8), st(5), -1e-5, 1.0 + 1e-6, ALU.mult, ALU.add))
            vch.wait()
            vch.put(V.reciprocal(st(9), st(8)))
            vch.wait()
        V.engine_nop()
        V.claim(g("s_hop"))
        g("s_hop").wait(SC)
        sch.wait()
        if want_rf1:
            sch.put(SC.activation(st(6), st(5), AF.Sqrt))
            sch.pute(SC.activation(st(10), st(9), AF.Sqrt), g("s_hop2"))
        else:
            sch.pute(SC.activation(st(6), st(5), AF.Sqrt), g("s_hop2"))
        g("s_hop2").wait(V)
        vch.put(V.tensor_scalar(st(7), st(2), st(6), -1.0, ALU.mult, ALU.mult))
        vch.wait()
        if xn_dst is not None:
            ins = V.tensor_scalar(xn_dst, src_ap, st(6), st(7), ALU.mult, ALU.add)
            vch.pute(ins, g("s_xn"))

    Tps = PSC[:, :].bitcast(BF16)
    ident32 = sbuf("ident32_s", [128, 128], F32)
    g("s_cg").inc(GP.dma_start(out=ident32[:], in_=ident_d), 16)
    tstat = sbuf("tstat", [16, 256], F32)
    negBbf = sbuf("negBbf", [1, 4], BF16)

    def transpose16(src2048_bf, n=16):
        last = None
        for c in range(n):
            last = TE.transpose(Tps[:, 128 * c:128 * (c + 1)],
                                src2048_bf[:, 128 * c:128 * (c + 1)], ident[:])
        return last

    def emit_B_assembly(branch):
        ix = 0 if branch == "ffn" else 1
        qcol = qssh[:, ix, :]
        nk = 4 if branch == "ffn" else 1
        kcol = kssh[:, 0:nk] if branch == "ffn" else kssh[:, 4:5]
        vch.wait(); sch.wait()
        V.engine_nop()
        V.claim(g("s_gen")); g("s_gen").wait(TE)
        TE.transpose(PSB[0:16, 0:128], qcol, ident32[:])
        i1 = TE.transpose(PSB[0:nk, 128:256], kcol, ident32[:])
        g("s_gen").inc(i1); g("s_gen").wait(V)
        vch.put(V.tensor_copy(tstat[:, 0:128], PSB[0:16, 0:128]))
        vch.put(V.tensor_copy(tstat[0:nk, 128:256], PSB[0:nk, 128:256]))
        vch.wait()
        vch.put(V.reduce_max(tstat[:, 140:141], tstat[:, 0:128], axis=AX.X))
        vch.put(V.reduce_max(tstat[0:nk, 142:143], tstat[0:nk, 128:256], axis=AX.X))
        vch.wait()
        V.engine_nop()
        V.claim(g("s_gen")); g("s_gen").wait(TE)
        TE.transpose(PSB[0:1, 0:16], tstat[:, 140:141], ident32[0:16, 0:16])
        i2 = TE.transpose(PSB[0:1, 16:16 + nk], tstat[0:nk, 142:143], ident32[0:nk, 0:nk])
        g("s_gen").inc(i2); g("s_gen").wait(V)
        vch.put(V.tensor_copy(brow_scr[0:1, 0:16], PSB[0:1, 0:16]))
        vch.put(V.tensor_copy(brow_scr[0:1, 16:16 + nk], PSB[0:1, 16:16 + nk]))
        vch.wait()
        if branch == "ffn":
            for a2 in range(4):
                vch.put(V.tensor_scalar(brow_scr[0:1, a2 * 4:a2 * 4 + 4], brow_scr[0:1, a2 * 4:a2 * 4 + 4],
                                        brow_scr[0:1, 16 + a2:17 + a2], None, ALU.mult))
        else:
            vch.put(V.tensor_scalar(brow_scr[0:1, 0:16], brow_scr[0:1, 0:16],
                                    brow_scr[0:1, 16:17], None, ALU.mult))
        vch.wait()
        V.engine_nop()
        V.claim(g("s_hop")); g("s_hop").wait(SC)
        sch.wait()
        sch.pute(SC.activation(brow_scr[0:1, 0:16], brow_scr[0:1, 0:16], AF.Sqrt), g("s_hop2"))
        g("s_hop2").wait(V)
        vch.put(V.tensor_scalar(brow_scr[0:1, 0:16], brow_scr[0:1, 0:16], -ISQ, None, ALU.mult))
        vch.wait()
        for gg in range(2):
            vch.put(V.tensor_reduce(brow_scr[0:1, 20 + gg:21 + gg], brow_scr[0:1, 8 * gg:8 * gg + 8],
                                    AX.X, ALU.min))
        vch.wait()
        vch.put(V.tensor_copy(negBbf[0:1, 2 * ix:2 * ix + 2], brow_scr[0:1, 20:22]))
        vch.wait()
        V.engine_nop()
        V.claim(g("s_gen")); g("s_gen").wait(TE)
        TE.ldweights(ones_row[0:1, :])
        i3 = None
        for gg in range(2):
            i3 = TE.matmul(PSB[:, 32 + gg:33 + gg], ones_row[0:1, :], negBbf[0:1, 2 * ix + gg:2 * ix + gg + 1],
                           start=True, stop=True, skip_group_check=True)
        g("s_gen").inc(i3); g("s_gen").wait(V)
        vch.put(V.tensor_copy(biasB[:, 2 * ix:2 * ix + 2], PSB[:, 32:34]))
        vch.wait()

    def emit_qbuild(branch, wbig, qT_arr, xb, xnscr, q_f, q_sw, qr1, q_bf,
                    rq_buf, junkL, qnwbc=None, h0=0, nh=16):
        nc_cols = nh * 128
        nch = nc_cols // 512
        ffn = branch == "ffn"
        sxp = (g("s_x0"), g("s_x1"))
        s_xn, s_ss = g("s_xn"), g("s_ss")
        ln_done = []
        qred_done = []
        rope_done = []
        ta_done = []
        pe_done = []
        tb_done = []

        def load(t):
            par = t % 2
            sx = sxp[par]
            if t >= 2:
                s_xn.wait(SY, ln_done[t - 2][0])
                s_ss.wait(SY, ln_done[t - 2][1])
                g("s_qb").wait(SY, rope_done[t - 2])
            sx.inc(SY.dma_start(out=xb[:, par, :], in_=x_own[128 * t:128 * (t + 1), :]), 16)
            if ffn:
                sx.inc(SY.dma_start(out=rq_buf[:, par, 0, 0:64], in_=cosFq_d[128 * t:128 * (t + 1), :]), 16)
                sx.inc(SY.dma_start(out=rq_buf[:, par, 1, 0:64], in_=sinFq_d[128 * t:128 * (t + 1), :]), 16)
            else:
                sx.inc(SY.dma_start(out=rq_buf[:, par, 0, :], in_=cosIq_d[128 * t:128 * (t + 1), :]), 16)
                sx.inc(SY.dma_start(out=rq_buf[:, par, 1, :], in_=sinIq_d[128 * t:128 * (t + 1), :]), 16)
            xload_done.append(sx.n)

        xload_done = []
        load(0)
        if S > 1:
            load(1)
        for t in range(S):
            par = t % 2
            xslice = xb[:, par, :]
            sxp[par].wait(V, xload_done[t]); sxp[par].wait(SC, xload_done[t])
            if t >= 2:
                g("s_Ta").wait(V, ta_done[t - 2])
            if t >= 1:
                g("s_kssv").wait(SC, qred_done[t - 1])
            emit_ln(par, xslice, want_rf1=ffn, xn_dst=None, junkL=junkL)
            st = lambda c: stat[:, par, c:c + 1]
            if ffn:
                vch.put(V.tensor_scalar(st(6), st(6), st(10), None, ALU.mult))
                vch.put(V.tensor_scalar(st(7), st(7), st(10), None, ALU.mult))
                vch.wait()
            vch.pute(V.tensor_scalar(xnscr[:, par, :], xslice, st(6), st(7), ALU.mult, ALU.add), g("s_xn"))
            ln_done.append((s_xn.n, s_ss.n))
            g("s_xn").wait(TE)
            if t >= 1:
                g("s_TevB").wait(TE, tb_done[t - 1])
            g("s_Ta").inc(transpose16(xnscr[:, par, :]))
            ta_done.append(g("s_Ta").n)
            g("s_Ta").wait(SC)
            sch.wait()
            sch.pute(SC.copy(xnT_scr[:, :, :].rearrange("p c q -> p (c q)"), Tps[:, :]), g("s_TevA"))
            g("s_TevA").wait(TE)
            if t >= 1:
                g("s_pe").wait(TE, pe_done[t - 1])
            last = None
            for c in range(16):
                TE.ldweights(xnT_scr[:, c, :])
                for b in range(nch):
                    last = TE.matmul(PSA_flat[:, 512 * b:512 * (b + 1)], xnT_scr[:, c, :],
                                     wbig[:, c, 512 * b:512 * (b + 1)],
                                     start=(c == 0), stop=False, skip_group_check=True)
            if not ffn:
                TE.ldweights(ones_row[0:1, :])
                for b in range(nch):
                    last = TE.matmul(PSA_flat[:, 512 * b:512 * (b + 1)], ones_row[0:1, :],
                                     brow_d[0:1, 128 * h0 + 512 * b:128 * h0 + 512 * (b + 1)],
                                     start=False, stop=True, skip_group_check=True)
            g("s_pj").inc(last)
            g("s_pj").wait(SC)
            sch.wait()
            if ffn:
                sch.put(SC.activation(junkL[:, 0:nc_cols], PSA_flat[:, 0:nc_cols], AF.Square,
                                      accum_out=st(11)))
            sch.pute(SC.copy(q_f[:, 0:nc_cols], PSA_flat[:, 0:nc_cols]), g("s_pe"))
            pe_done.append(g("s_pe").n)
            g("s_pe").wait(V)
            if ffn:
                g("s_ss").wait(V)
                sch_count_hop = g("s_hop")
                vch.put(V.tensor_scalar(st(3), st(11), 1.0 / D, 1e-6, ALU.mult, ALU.add))
                vch.wait()
                vch.put(V.reciprocal(st(4), st(3)))
                vch.wait()
                V.engine_nop(); V.claim(sch_count_hop)
                sch_count_hop.wait(SC)
                sch.wait()
                sch.pute(SC.activation(st(11), st(4), AF.Sqrt), g("s_hop2"))
                g("s_hop2").wait(V)
                vch.put(V.scalar_tensor_tensor(q_f[:, 0:nc_cols], q_f[:, 0:nc_cols], st(11), qnwbc[:, :], ALU.mult, ALU.mult))
                vch.wait()
                qsrc = q_f
            else:
                qsrc = q_f
            V.engine_nop()
            V.claim(g("s_gen"))
            g("s_gen").wait(SC)
            sch.wait()
            sch.pute(SC.activation(junkL[:, 0:nc_cols], qsrc[:, 0:nc_cols], AF.Square), g("s_hop"))
            g("s_hop").wait(V)
            vch.put(V.tensor_reduce(qtmp16[:, h0:h0 + nh].rearrange("p (h o) -> p h o", o=1),
                                    junkL[:, 0:nc_cols].rearrange("p (h d) -> p h d", h=nh), AX.X, ALU.add))
            vch.wait()
            vch.put(V.tensor_max(qssh[:, 0 if ffn else 1, h0:h0 + nh],
                                 qssh[:, 0 if ffn else 1, h0:h0 + nh], qtmp16[:, h0:h0 + nh]))
            vch.pute(V.engine_nop(), g("s_kssv"))
            qred_done.append(g("s_kssv").n)
            q3 = qsrc[:, 0:nc_cols].rearrange("p (h d) -> p h d", h=nh)
            qb3 = q_bf[:, 0:nc_cols].rearrange("p (h d) -> p h d", h=nh)
            if ffn:
                rot = q3[:, :, 0:64]
                sw_src = rot.rearrange("p h (b c) -> p h b c", b=2)[:, :, ::-1, :]
                vch.put(V.tensor_copy(q_sw[:, 0:nh, 0:64].rearrange("p h (b c) -> p h b c", b=2), sw_src))
                cosb = rq_buf[:, par, 0, 0:64].unsqueeze(1).broadcast_to([128, nh, 64])
                sinb = rq_buf[:, par, 1, 0:64].unsqueeze(1).broadcast_to([128, nh, 64])
                vch.put(V.tensor_mul(qr1[:, 0:nh, 0:64], rot, cosb))
                vch.wait()
                vch.put(V.tensor_mul(q_sw[:, 0:nh, 0:64], q_sw[:, 0:nh, 0:64], sinb))
                vch.wait()
                vch.put(V.tensor_add(qb3[:, :, 0:64], qr1[:, 0:nh, 0:64], q_sw[:, 0:nh, 0:64]))
                vch.put(V.tensor_copy(qb3[:, :, 64:128], q3[:, :, 64:128]))
            else:
                sw_src = qsrc[:, 0:nc_cols].rearrange("p (u two) -> p u two", two=2)[:, :, ::-1]
                vch.put(V.tensor_copy(q_sw[:, 0:nh, :].rearrange("p h d -> p (h d)").rearrange("p (u two) -> p u two", two=2), sw_src))
                cosb = rq_buf[:, par, 0, :].unsqueeze(1).broadcast_to([128, nh, 128])
                sinb = rq_buf[:, par, 1, :].unsqueeze(1).broadcast_to([128, nh, 128])
                vch.put(V.tensor_mul(qr1[:, 0:nh, :], q3, cosb))
                vch.wait()
                vch.put(V.tensor_mul(q_sw[:, 0:nh, :], q_sw[:, 0:nh, :], sinb))
                vch.wait()
                vch.put(V.tensor_add(qb3[:, :, :], qr1[:, 0:nh, :], q_sw[:, 0:nh, :]))
            vch.pute(V.engine_nop(), g("s_qb"))
            rope_done.append(g("s_qb").n)
            g("s_qb").wait(TE)
            g("s_Tb").inc(transpose16(q_bf[:, 0:nc_cols], n=nh))
            g("s_Tb").wait(SC)
            sch.wait()
            sch.pute(SC.copy(qT_arr[:, t, h0:h0 + nh, :].rearrange("p h q -> p (h q)"),
                             Tps[:, 0:nc_cols]), g("s_TevB"))
            tb_done.append(g("s_TevB").n)
            if t + 2 < S:
                load(t + 2)

    def emit_p3_full(wlat, wkv, knwbc, kT_g0, v_g0, kvsc, xb, xnscr,
                     kf32, ksw, rop1, rop2, knb, rfb, junkL, junkK, kTo, vo):
        nt = NT
        sxp = (g("s_x0"), g("s_x1"))
        s_xn, s_ss = g("s_xn"), g("s_ss")
        bTa, bTevA, bpj, bpe = g("s_Ta").n, g("s_TevA").n, g("s_pj").n, g("s_pe").n
        bknb, bTb, bTevB = g("s_knb").n, g("s_Tb").n, g("s_TevB").n
        bxn = s_xn.n
        ln_done, rope_done, kss_done, kvm_done, xload_done, kvd_done = [], [], [], [], [], []
        ks_done, vs_done = [], []
        sksp = (g("s_ks0"), g("s_ks1"))
        svsp = (g("s_vs0"), g("s_vs1"))

        def load(j):
            par = j % 2
            sx = sxp[par]
            if j >= 2:
                s_xn.wait(SY, ln_done[j - 2][0])
                s_ss.wait(SY, ln_done[j - 2][1])
                g("s_knb").wait(SY, bknb + j - 1)
            sx.inc(SY.dma_start(out=xb[:, par, :], in_=x_full[128 * j:128 * (j + 1), :]), 16)
            sx.inc(SY.dma_start(out=rfb[:, par, 0, :], in_=cosF_d[128 * j:128 * (j + 1), :]), 16)
            sx.inc(SY.dma_start(out=rfb[:, par, 1, :], in_=sinF_d[128 * j:128 * (j + 1), :]), 16)
            xload_done.append(sx.n)

        def tens_tile(j_expr, par):
            TE.wait_ge(g("s_xn").h, bxn + j_expr + 1)
            TE.wait_ge(g("s_TevB").h, bTevB + j_expr)
            transpose16(xnscr[:, par, :]).then_inc(g("s_Ta").h, 1)
            TE.wait_ge(g("s_TevA").h, bTevA + j_expr + 1)
            TE.wait_ge(g("s_pe").h, bpe + 2 * j_expr)
            for c in range(16):
                TE.ldweights(xnT_scr[:, c, :])
                TE.matmul(PSA[:, par, 0:512], xnT_scr[:, c, :], wkv[:, c, 0:512],
                          start=(c == 0), stop=False, skip_group_check=True)
                TE.matmul(PSA[:, par, 512:1024], xnT_scr[:, c, :], wkv[:, c, 512:1024],
                          start=(c == 0), stop=False, skip_group_check=True)
                TE.matmul(PSB[:, 512 * par:512 * par + 128], xnT_scr[:, c, :], wlat[:, c, :],
                          start=(c == 0), stop=False, skip_group_check=True)
            TE.ldweights(ones_row[0:1, :])
            TE.matmul(PSB[:, 512 * par:512 * par + 128], ones_row[0:1, :], brow_lat[0:1, :],
                      start=False, stop=True, skip_group_check=True).then_inc(g("s_pj").h, 1)
            TE.wait_ge(g("s_knb").h, bknb + j_expr + 1)
            TE.wait_ge(g("s_TevA").h, bTevA + j_expr + 1)
            tl = None
            for a in range(4):
                tl = TE.transpose(Tps[:, 128 * a:128 * (a + 1)], knb[:, par, 128 * a:128 * (a + 1)], ident[:])
            tl.then_inc(g("s_Tb").h, 1)

        SY.sem_inc(g("s_pe").h, 2); g("s_pe").n += 2
        g("s_cg").wait(TE)
        g("s_wf").wait(TE)
        with reg_reclaim(), TE.Fori(0, nt // 2) as m:
            tens_tile(m * 2, 0)
            tens_tile(m * 2 + 1, 1)
        g("s_Ta").n += nt
        g("s_pj").n += nt
        g("s_Tb").n += nt

        load(0)
        load(1)
        st_of = lambda par, c: stat[:, par, c:c + 1]
        s1_done, s3_done = [], []

        def V1(j):
            par = j % 2
            sxp[par].wait(V, xload_done[j])
            vch.put(V.reduce_sum(st_of(par, 0), xb[:, par, :], axis=AX.X))

        def S1(j):
            par = j % 2
            sxp[par].wait(SC, xload_done[j])
            sch.wait()
            sch.pute(SC.activation(junkL[:], xb[:, par, :], AF.Square, accum_out=st_of(par, 1)),
                     g("s_ss"))
            s1_done.append(s_ss.n)

        def V2(j):
            par = j % 2
            st = lambda c: st_of(par, c)
            s_ss.wait(V, s1_done[j])
            vch.put(V.tensor_scalar(st(2), st(0), 1.0 / D, None, ALU.mult))
            vch.put(V.tensor_scalar(st(3), st(1), 1.0 / D, None, ALU.mult))
            vch.put(V.scalar_tensor_tensor(st(4), st(2), st(2), st(3), ALU.mult, ALU.subtract))
            vch.put(V.tensor_scalar(st(4), st(4), -1.0, 1e-5, ALU.mult, ALU.add))
            vch.put(V.reciprocal(st(5), st(4)))
            vch.put(V.tensor_scalar(st(8), st(5), -1e-5, 1.0 + 1e-6, ALU.mult, ALU.add))
            vch.put(V.reciprocal(st(9), st(8)))
            V.engine_nop()
            V.claim(g("s_hop"))

        def S2(j):
            par = j % 2
            g("s_hop").wait(SC)
            sch.wait()
            sch.put(SC.activation(st_of(par, 6), st_of(par, 5), AF.Sqrt))
            sch.pute(SC.activation(st_of(par, 10), st_of(par, 9), AF.Sqrt), g("s_hop2"))

        def V3(j):
            par = j % 2
            st = lambda c: st_of(par, c)
            g("s_hop2").wait(V)
            vch.put(V.tensor_scalar(st(7), st(2), st(6), -1.0, ALU.mult, ALU.mult))
            if j >= 2:
                g("s_Ta").wait(V, bTa + j - 1)
            vch.pute(V.tensor_scalar(xnscr[:, par, :], xb[:, par, :], st(6), st(7),
                                     ALU.mult, ALU.add), g("s_xn"))
            ln_done.append((s_xn.n, s1_done[j]))

        def S3(j):
            par = j % 2
            st = lambda c: st_of(par, c)
            g("s_Ta").wait(SC, bTa + j + 1)
            sch.wait()
            sch.pute(SC.copy(xnT_scr[:, :, :].rearrange("p c q -> p (c q)"), Tps[:, :]), g("s_TevA"))
            SC.wait_ge(g("s_pj").h, bpj + j + 1)
            if j >= 2:
                (g("s_kvd0"), g("s_kvd1"))[par].wait(SC, kvd_done[j - 2])
            else:
                (g("s_kvd0"), g("s_kvd1"))[par].wait(SC)
            sch.pute(SC.copy(kvsc[:, par, :], PSB[:, 512 * par:512 * par + 128]), g("s_gen"))
            g("s_gen").wait(SY)
            skv = (g("s_kvd0"), g("s_kvd1"))[par]
            skv.inc(SY.dma_start(out=kv_dram[128 * j:128 * (j + 1), :], in_=kvsc[:, par, :]), 16)
            kvd_done.append(skv.n)
            if j >= 2:
                g("s_kssv").wait(SC, kss_done[j - 2])
            sch.put(SC.activation(junkK[:, par, 0:128], PSB[:, 512 * par:512 * par + 128],
                                  AF.Square, accum_out=st(12)))
            sch.put(SC.copy(kf32[:, par, :], PSA[:, par, 0:512]))
            sch.pute(SC.activation(junkK[:, par, :], PSA[:, par, 0:512], AF.Square,
                                   accum_out=st(11)), g("s_pe"))
            s3_done.append(g("s_pe").n)

        def V4(j):
            par = j % 2
            st = lambda c: st_of(par, c)
            g("s_pe").wait(V, s3_done[j])
            vch.put(V.tensor_scalar(v_g0[:, j, :, :].rearrange("p a d -> p (a d)"),
                                    PSA[:, par, 512:768], st(10), None, ALU.mult))
            if j >= 2:
                svsp[par].wait(V, vs_done[j - 2])
            else:
                svsp[par].wait(V)
            vch.pute(V.tensor_scalar(vo[:, par, :], PSA[:, par, 768:1024], st(10), None, ALU.mult),
                     g("s_vo"))
            V.engine_nop()
            V.claim(g("s_pe"))
            g("s_vo").wait(SY)
            svs = svsp[par]
            svs.inc(SY.dma_start(out=v1_dram[:, j, :, :].rearrange("p a d -> p (a d)"),
                                 in_=vo[:, par, :]), 16)
            vs_done.append(svs.n)
            vch.put(V.tensor_max(kssh[:, 4:5], kssh[:, 4:5], st(12)))
            vch.pute(V.engine_nop(), g("s_kvmax"))
            kvm_done.append(g("s_kvmax").n)
            vch.put(V.tensor_scalar(st(13), st(10), st(10), None, ALU.mult))
            vch.put(V.tensor_scalar(st(14), st(11), st(13), None, ALU.mult))
            vch.put(V.tensor_scalar(st(14), st(14), 1.0 / KW, 1e-6, ALU.mult, ALU.add))
            vch.put(V.reciprocal(st(15), st(14)))
            V.engine_nop()
            V.claim(g("s_hop"))

        def S4(j):
            par = j % 2
            g("s_hop").wait(SC)
            sch.wait()
            sch.pute(SC.activation(st_of(par, 14), st_of(par, 15), AF.Sqrt), g("s_hop2"))

        def V5(j):
            par = j % 2
            st = lambda c: st_of(par, c)
            g("s_hop2").wait(V)
            vch.put(V.tensor_scalar(st(15), st(14), st(10), None, ALU.mult))
            vch.put(V.scalar_tensor_tensor(kf32[:, par, :], kf32[:, par, :], st(15), knwbc[:, :],
                                           ALU.mult, ALU.mult))
            V.engine_nop()
            V.claim(g("s_hop"))

        def S5(j):
            par = j % 2
            g("s_hop").wait(SC)
            sch.wait()
            sch.pute(SC.activation(junkK[:, par, :], kf32[:, par, :], AF.Square), g("s_hop2"))

        def V6(j):
            par = j % 2
            g("s_hop2").wait(V)
            vch.put(V.tensor_reduce(ktmp4[:, :].rearrange("p (a o) -> p a o", o=1),
                                    junkK[:, par, :].rearrange("p (a d) -> p a d", a=4),
                                    AX.X, ALU.add))
            vch.put(V.tensor_max(kssh[:, 0:4], kssh[:, 0:4], ktmp4[:, :]))
            vch.pute(V.engine_nop(), g("s_kssv"))
            kss_done.append(g("s_kssv").n)
            k3 = kf32[:, par, :].rearrange("p (a d) -> p a d", a=4)
            kb3 = knb[:, par, :].rearrange("p (a d) -> p a d", a=4)
            rot = k3[:, :, 0:64]
            sw_src = rot.rearrange("p a (b c) -> p a b c", b=2)[:, :, ::-1, :]
            vch.put(V.tensor_copy(ksw[:, :, :].rearrange("p a (b c) -> p a b c", b=2), sw_src))
            cosb = rfb[:, par, 0, :].unsqueeze(1).broadcast_to([128, 4, 64])
            sinb = rfb[:, par, 1, :].unsqueeze(1).broadcast_to([128, 4, 64])
            vch.put(V.tensor_mul(rop1[:, :, :], rot, cosb))
            vch.put(V.tensor_mul(rop2[:, :, :], ksw[:, :, :], sinb))
            vch.put(V.tensor_add(kb3[:, :, 0:64], rop1[:, :, :], rop2[:, :, :]))
            vch.put(V.tensor_copy(kb3[:, :, 64:128], k3[:, :, 64:128]))
            vch.pute(V.engine_nop(), g("s_knb"))
            rope_done.append(g("s_knb").n)

        def S6(j):
            par = j % 2
            SC.wait_ge(g("s_Tb").h, bTb + j + 1)
            if j >= 2:
                sksp[par].wait(SC, ks_done[j - 2])
            else:
                sksp[par].wait(SC)
            sch.wait()
            sch.pute(SC.copy(kTo[:, par, :], Tps[:, 256:512]), g("s_ko"))
            sch.pute(SC.copy(kT_g0[:, :, j, :], Tps[:, 0:256].rearrange("p (a q) -> p a q", a=2)),
                     g("s_TevB"))
            g("s_ko").wait(SY)
            sks = sksp[par]
            sks.inc(SY.dma_start(out=kT1_dram[:, j, :, :].rearrange("p a q -> p (a q)"),
                                 in_=kTo[:, par, :]), 16)
            ks_done.append(sks.n)

        for j0 in range(0, nt, 2):
            j1 = j0 + 1
            V1(j0); V1(j1)
            S1(j0); S1(j1)
            V2(j0); V2(j1)
            S2(j0); S2(j1)
            V3(j0); V3(j1)
            S3(j0); V4(j0); S4(j0); V5(j0); S5(j0); V6(j0); S6(j0)
            S3(j1); V4(j1); S4(j1); V5(j1); S5(j1); V6(j1); S6(j1)
            if j0 + 2 < nt:
                load(j0 + 2)
            if j1 + 2 < nt:
                load(j1 + 2)


    def emit_attn_pass(br, t, g2, lo, hi, Et, qT_arr, spill_O):
        ffn = br == "ffn"
        sfx = Et - NSFX
        ncol = 2 * (0 if ffn else 1)
        assert lo == 0
        qk, exp, mk, pv = g("s_qk"), g("s_exp"), g("s_mk"), g("s_pv")
        stgp = (g("s_stg0"), g("s_stg1"), g("s_stg2"), g("s_stg3"))
        bqk, bexp, bmk, bpv = qk.n, exp.n, mk.n, pv.n
        bstg = tuple(s.n for s in stgp)
        qk_after = lambda j: bqk + (j - lo + 1)
        exp_after = lambda j: bexp + (j - lo + 1)
        pv_after = lambda j: bpv + (j - lo + 1)
        stg_after = lambda j: bstg[(j - lo) % 4] + 32 * ((j - lo) // 4 + 1)

        def ksrc(j):
            if ffn:
                return kT_g0v[:, :, j, :] if g2 == 0 else kT1_dram[:, j, :, :].rearrange("p a q -> p (a q)")
            return kvTv[:, j, :]

        def vsrc(j):
            if ffn:
                return v_g0v[:, j, :, :] if g2 == 0 else v1_dram[:, j, :, :].rearrange("p a d -> p (a d)")
            return kv_rawv[:, j, :]

        for j in range(lo, hi):
            slot = (j - lo) % 4
            stg = stgp[slot]
            if j - 4 >= lo:
                qk.wait(SY, qk_after(j - 4))
                pv.wait(SY, pv_after(j - 4))
            else:
                qk.wait(SY, bqk)
                pv.wait(SY, bpv)
            if ffn:
                stg.inc(SY.dma_start(out=kstg[:, slot, :, :].rearrange("p a q -> p (a q)"), in_=ksrc(j)), 16)
                stg.inc(SY.dma_start(out=vstg[:, slot, :, :].rearrange("p a q -> p (a q)"), in_=vsrc(j)), 16)
            else:
                stg.inc(SY.dma_start(out=kstg[:, slot, 0, :], in_=ksrc(j)), 16)
                stg.inc(SY.dma_start(out=vstg[:, slot, 0, :], in_=vsrc(j)), 16)

        def rhs_q(a):
            return qT_arr[:, t, 8 * g2 + 4 * a:8 * g2 + 4 * a + 4, :].rearrange("p h q -> p (h q)")

        def emit_qk(j, par, slot, stg_wait=None, exp_wait=None):
            if stg_wait is not None:
                TE.wait_ge(stgp[slot].h, stg_wait)
                TE.wait_ge(exp.h, exp_wait)
            else:
                TE.wait_ge(stgp[slot].h, stg_after(j))
                if j - 2 >= lo:
                    TE.wait_ge(exp.h, exp_after(j - 2))
            last = None
            if ffn:
                for a in range(2):
                    TE.ldweights(kstg[:, slot, a, :])
                    last = TE.matmul(PSA[:, par, 512 * a:512 * (a + 1)], kstg[:, slot, a, :], rhs_q(a),
                                     start=True, stop=True, skip_group_check=True)
            else:
                TE.ldweights(kstg[:, slot, 0, :])
                for a in range(2):
                    last = TE.matmul(PSA[:, par, 512 * a:512 * (a + 1)], kstg[:, slot, 0, :], rhs_q(a),
                                     start=True, stop=True, skip_group_check=True)
            last.then_inc(qk.h, 1)

        def emit_pv(j, par, slot, first, masked, exp_wait=None):
            if exp_wait is not None:
                TE.wait_ge(exp.h, exp_wait)
            elif masked:
                TE.wait_ge(mk.h, bmk + (j - max(sfx, lo) + 1))
            else:
                TE.wait_ge(exp.h, exp_after(j))
            last = None
            if ffn:
                for a in range(2):
                    TE.ldweights(vstg[:, slot, a, :])
                    TE.matmul(PSB[:, 512 * a:512 * (a + 1)], vstg[:, slot, a, :],
                              pbuf[:, par, 512 * a:512 * (a + 1)],
                              start=first, stop=False, skip_group_check=True)
            else:
                TE.ldweights(vstg[:, slot, 0, :])
                for a in range(2):
                    TE.matmul(PSB[:, 512 * a:512 * (a + 1)], vstg[:, slot, 0, :],
                              pbuf[:, par, 512 * a:512 * (a + 1)],
                              start=first, stop=False, skip_group_check=True)
            TE.ldweights(ones_col[:, 0:1])
            for a in range(2):
                last = TE.matmul(PSC[0:1, 512 * a:512 * (a + 1)], ones_col[:, 0:1],
                                 pbuf[:, par, 512 * a:512 * (a + 1)],
                                 start=first, stop=False, skip_group_check=True)
            last.then_inc(pv.h, 1)

        g("s_sp").wait(TE)
        g("s_dnc").wait(TE)
        nfori = 4 * max(0, (min(sfx, hi) - (lo + 2)) // 4)
        fori_end = lo + 2 + nfori
        emit_qk(lo, lo % 2, 0)
        if lo + 1 < hi:
            emit_qk(lo + 1, (lo + 1) % 2, 1)
        emit_pv(lo, lo % 2, 0, True, lo >= sfx)
        if nfori > 0:
            with reg_reclaim(), TE.Fori(0, nfori // 4) as m:
                m4 = m * 4
                m32 = m * 32
                for k in range(4):
                    emit_qk(None, k % 2, (2 + k) % 4,
                            stg_wait=bstg[(2 + k) % 4] + (32 if k < 2 else 64) + m32,
                            exp_wait=bexp + 1 + k + m4)
                    emit_pv(None, (1 + k) % 2, (1 + k) % 4, False, False,
                            exp_wait=bexp + 2 + k + m4)
            qk.n += nfori
            pv.n += nfori
        for j in range(fori_end, hi):
            emit_qk(j, j % 2, (j - lo) % 4)
            emit_pv(j - 1, (j - 1) % 2, (j - 1 - lo) % 4, False, (j - 1) >= sfx)
        emit_pv(hi - 1, (hi - 1) % 2, (hi - 1 - lo) % 4, False, (hi - 1) >= sfx)
        qk.n = qk_after(hi - 1)
        pv.n = pv_after(hi - 1)

        for j in range(lo, hi):
            par = j % 2
            SC.wait_ge(qk.h, qk_after(j))
            if j - 2 >= lo:
                SC.wait_ge(pv.h, pv_after(j - 2))
            else:
                SC.wait_ge(pv.h, bpv)
            SC.activation(pbuf[:, par, :], PSA[:, par, :], AF.Exp,
                          bias=biasB[:, ncol + g2:ncol + g2 + 1], scale=ISQ)
            SC.claim(exp)
            if j >= sfx:
                V.wait_ge(exp.h, exp.n)
                V.tensor_mul(pbuf[:, par, :], pbuf[:, par, :], masksv[:, j - sfx, :])
                V.claim(mk)
        V.wait_ge(pv.h, pv.n)
        if spill_O is not None:
            vch.put(V.tensor_add(spill_O, spill_O, PSB[:, :]))
            vch.put(V.tensor_copy(lrow0[0:1, :], PSC[0:1, 0:1024]))
            vch.pute(V.engine_nop(), g("s_gen"))
            g("s_gen").wait(TE)
            il = None
            for h4 in range(8):
                il = TE.transpose(PSA_flat[:, h4:h4 + 1], lrow0[0:1, 128 * h4:128 * (h4 + 1)],
                                  ident32[0:1, 0:1])
            g("s_gen").inc(il)
            g("s_gen").wait(V)
            vch.put(V.tensor_add(lacc_tok[:, t, 8 * g2:8 * g2 + 8],
                                 lacc_tok[:, t, 8 * g2:8 * g2 + 8], PSA_flat[:, 0:8]))
            vch.pute(V.engine_nop(), g("s_sp"))

    def emit_dance(t, g2, o_src_psum, Osc_v):
        PSAbf = PSA_flat.bitcast(BF16)
        if o_src_psum:
            V.wait_ge(g("s_pv").h, g("s_pv").n)
            vch.put(V.tensor_copy(obuf[:, 0:1024], PSB[:, :]))
            vch.put(V.tensor_copy(lrow0[0:1, :], PSC[0:1, 0:1024]))
            vch.wait()
        else:
            vch.put(V.tensor_copy(obuf[:, 0:1024],
                                  Oaccv[:, t, 8 * g2:8 * g2 + 8, :].rearrange("p h q -> p (h q)")))
            vch.wait()
        vch.pute(V.engine_nop(), g("s_dnc"))
        g("s_dnc").wait(TE)
        i1 = None
        for h4 in range(8):
            i1 = TE.transpose(PSAbf[:, 2048 + 128 * h4:2048 + 128 * (h4 + 1)],
                              obuf[:, 128 * h4:128 * (h4 + 1)], ident[:])
        if o_src_psum:
            for h4 in range(8):
                i1 = TE.transpose(PSA_flat[:, h4:h4 + 1], lrow0[0:1, 128 * h4:128 * (h4 + 1)],
                                  ident32[0:1, 0:1])
        g("s_dnc").inc(i1)
        g("s_dnc").wait(V)
        if o_src_psum:
            vch.put(V.tensor_copy(lt_scr[:, 0:8], PSA_flat[:, 0:8]))
        else:
            vch.put(V.tensor_copy(lt_scr[:, 0:8], lacc_tok[:, t, 8 * g2:8 * g2 + 8]))
        vch.wait()
        vch.put(V.reciprocal(rl_scr[:, 0:8], lt_scr[:, 0:8]))
        vch.wait()
        for h4 in range(8):
            ii = V.tensor_scalar(
                Osc_v[:, t, 8 * g2 + h4, :], PSAbf[:, 2048 + 128 * h4:2048 + 128 * (h4 + 1)],
                rl_scr[:, h4:h4 + 1], None, ALU.mult)
            if h4 == 7:
                vch.pute(ii, g("s_dnc"))

    def emit_poproj(kind, Osc_v, wbig, x1_v, ACC_v, xb, junkL):
        if kind == "o":
            hh, nco = 0, 2048
        else:
            hh, nco = int(kind[-1]), 1024
        ncb = nco // 512
        xld = []
        ln_done = []

        def load(t):
            par = t % 2
            if t >= 2:
                g("s_xn").wait(SY, ln_done[t - 2][0])
                g("s_ss").wait(SY, ln_done[t - 2][1])
            sx = (g("s_x0"), g("s_x1"))[par]
            sx.inc(SY.dma_start(out=xb[:, par, :], in_=x_own[128 * t:128 * (t + 1), :]), 16)
            xld.append(sx.n)

        if kind == "o":
            load(0)
            if S > 1:
                load(1)
        pj_done = []
        for t in range(S):
            par = t % 2
            vch.wait()
            if t >= 1:
                V.wait_ge(g("s_pj").h, pj_done[t - 1])
            vch.pute(V.tensor_copy(ot_stage[:, :], Osc_v[:, t, :, :].rearrange("p h d -> p (h d)")),
                     g("s_x2"))
            g("s_x2").wait(TE)
            g("s_TevA").wait(TE)
            g("s_Ta").inc(transpose16(ot_stage[:, :]))
            g("s_Ta").wait(SC)
            sch.wait()
            sch.pute(SC.copy(otT_scr[:, :, :].rearrange("p c q -> p (c q)"), Tps[:, :]), g("s_TevA"))
            g("s_TevA").wait(TE)
            g("s_pe").wait(TE)
            last = None
            for c in range(16):
                TE.ldweights(otT_scr[:, c, :])
                for b in range(ncb):
                    last = TE.matmul(PSA_flat[:, 512 * b:512 * (b + 1)], otT_scr[:, c, :],
                                     wbig[:, c, 512 * b:512 * (b + 1)],
                                     start=(c == 0), stop=False, skip_group_check=True)
            if kind != "o":
                TE.ldweights(ones_row[0:1, :])
                for b in range(ncb):
                    last = TE.matmul(PSA_flat[:, 512 * b:512 * (b + 1)], ones_row[0:1, :],
                                     brow_proj[0:1, 1024 * hh + 512 * b:1024 * hh + 512 * (b + 1)],
                                     start=False, stop=True, skip_group_check=True)
            g("s_pj").inc(last)
            pj_done.append(g("s_pj").n)
            if kind == "o":
                sx = (g("s_x0"), g("s_x1"))[par]
                sx.wait(V, xld[t]); sx.wait(SC, xld[t])
                emit_ln(par, xb[:, par, :], want_rf1=False, xn_dst=None, junkL=junkL)
                st = lambda c: stat[:, par, c:c + 1]
                vch.put(V.tensor_scalar(q_fv[:, :], xb[:, par, :], st(6), st(7), ALU.mult, ALU.add))
                vch.pute(V.engine_nop(), g("s_xn"))
                ln_done.append((g("s_xn").n, g("s_ss").n))
                g("s_pj").wait(V)
                vch.pute(V.tensor_add(x1_v[:, t, :], q_fv[:, :], PSA_flat[:, :]), g("s_pe"))
                if t + 2 < S:
                    load(t + 2)
            else:
                g("s_pj").wait(V)
                vch.wait()
                vch.pute(V.tensor_add(ACC_v[:, t, 1024 * hh:1024 * (hh + 1)],
                                      PSA_flat[:, 0:1024],
                                      x1_v[:, t, 1024 * hh:1024 * (hh + 1)]), g("s_pe"))

    def emit_pr(kv_rawv, kvTv, rib, rsw, rr1, rr2, rkb):
        rld = []
        rope_done = []

        def load(j):
            par = j % 2
            if j >= 2:
                g("s_knb").wait(SY, rope_done[j - 2])
            sx = (g("s_x0"), g("s_x1"))[par]
            sx.inc(SY.dma_start(out=rib[:, par, 0, :], in_=cosI_d[128 * j:128 * (j + 1), :]), 16)
            sx.inc(SY.dma_start(out=rib[:, par, 1, :], in_=sinI_d[128 * j:128 * (j + 1), :]), 16)
            rld.append(sx.n)

        load(0)
        if NT > 1:
            load(1)
        bTb, bTevB = g("s_Tb").n, g("s_TevB").n
        for j in range(NT):
            par = j % 2
            (g("s_x0"), g("s_x1"))[par].wait(V, rld[j])
            src = kv_rawv[:, j, :]
            sw_src = src.rearrange("p (u two) -> p u two", two=2)[:, :, ::-1]
            vch.wait()
            vch.put(V.tensor_copy(rsw[:, :].rearrange("p (u two) -> p u two", two=2), sw_src))
            vch.put(V.tensor_mul(rr1[:, :], src, rib[:, par, 0, :]))
            vch.wait()
            vch.put(V.tensor_mul(rr2[:, :], rsw[:, :], rib[:, par, 1, :]))
            vch.wait()
            if j >= 1:
                V.wait_ge(g("s_Tb").h, bTb + j)
            vch.pute(V.tensor_add(rkb[:, :], rr1[:, :], rr2[:, :]), g("s_knb"))
            rope_done.append(g("s_knb").n)
            if j + 2 < NT:
                load(j + 2)
            g("s_knb").wait(TE)
            TE.wait_ge(g("s_TevB").h, bTevB + j)
            TE.transpose(Tps[:, 0:128], rkb[:, :], ident[:]).then_inc(g("s_Tb").h, 1)
            g("s_Tb").n += 1
            SC.wait_ge(g("s_Tb").h, g("s_Tb").n)
            sch.wait()
            sch.pute(SC.copy(kvTv[:, j, :], Tps[:, 0:128]), g("s_TevB"))

    def emit_p5pre(x1_v, tT_v, ln2bc, junkL, tstage):
        bTa0 = g("s_Ta").n
        for t in range(S):
            par = t % 2
            st = lambda c: stat[:, par, c:c + 1]
            if t >= 1:
                g("s_kssv").wait(SC)
            g("s_sp").wait(SC)
            sch.wait()
            sch.pute(SC.activation(junkL[:], x1_v[:, t, :], AF.Square, accum_out=st(11)), g("s_ss"))
            g("s_ss").wait(V)
            vch.wait()
            vch.put(V.tensor_scalar(st(3), st(11), 1.0 / D, 1e-6, ALU.mult, ALU.add))
            vch.wait()
            vch.put(V.reciprocal(st(4), st(3)))
            vch.wait()
            V.engine_nop()
            V.claim(g("s_hop")); g("s_hop").wait(SC)
            sch.pute(SC.activation(st(5), st(4), AF.Sqrt), g("s_hop2"))
            g("s_hop2").wait(V)
            vch.put(V.tensor_copy(r2s[:, t:t + 1], st(5)))
            vch.wait()
            if t >= 1:
                g("s_Ta").wait(V, bTa0 + t)
            vch.pute(V.scalar_tensor_tensor(tstage[:, :], x1_v[:, t, :], st(5), ln2bc[:, :],
                                            ALU.mult, ALU.mult), g("s_kssv"))
            g("s_kssv").wait(TE)
            g("s_TevA").wait(TE)
            g("s_Ta").inc(transpose16(tstage[:, :]))
            g("s_Ta").wait(SC)
            sch.wait()
            sch.pute(SC.copy(tT_v[:, t, :, :].rearrange("p c q -> p (c q)"), Tps[:, :]), g("s_TevA"))
        for t in range(S):
            g("s_TevA").wait(TE)
            g("s_pe").wait(TE)
            last = None
            for c in range(16):
                TE.ldweights(tT_v[:, t, c, :])
                last = TE.matmul(PSB[:, 0:8], tT_v[:, t, c, :], rw_sb[:, c, :],
                                 start=(c == 0), stop=(c == 15), skip_group_check=True)
            g("s_pj").inc(last)
            g("s_pj").wait(SC)
            if t >= 1:
                g("s_sg").wait(SC)
            sch.wait()
            sch.pute(SC.activation(qtmp16[:, 0:8], PSB[:, 0:8], AF.Sigmoid), g("s_hop"))
            g("s_hop").wait(V)
            vch.wait()
            vch.put(V.tensor_add(qtmp16[:, 8:16], qtmp16[:, 0:8], rbias_bc[:, :]))
            vch.pute(V.engine_nop(), g("s_pe"))
            vch.wait()
            vch.put(V.reduce_max(lt_scr[:, 0:1], qtmp16[:, 8:16], axis=AX.X))
            vch.wait()
            vch.put(V.tensor_scalar(rl_scr[:, 0:8], qtmp16[:, 8:16], lt_scr[:, 0:1], None, ALU.is_ge))
            vch.wait()
            vch.put(V.scalar_tensor_tensor(rl_scr[:, 8:16], rl_scr[:, 0:8], -1e30, qtmp16[:, 8:16],
                                           ALU.mult, ALU.add))
            vch.wait()
            vch.put(V.reduce_max(lt_scr[:, 1:2], rl_scr[:, 8:16], axis=AX.X))
            vch.wait()
            vch.put(V.tensor_scalar(lt_scr[:, 8:16], rl_scr[:, 8:16], lt_scr[:, 1:2], None, ALU.is_ge))
            vch.wait()
            vch.put(V.tensor_add(rl_scr[:, 0:8], rl_scr[:, 0:8], lt_scr[:, 8:16]))
            vch.wait()
            vch.pute(V.tensor_mul(rl_scr[:, 0:8], rl_scr[:, 0:8], qtmp16[:, 0:8]), g("s_sg"))
            vch.wait()
            vch.put(V.reduce_sum(lt_scr[:, 2:3], rl_scr[:, 0:8], axis=AX.X))
            vch.wait()
            vch.put(V.reciprocal(lt_scr[:, 3:4], lt_scr[:, 2:3]))
            vch.wait()
            vch.put(V.tensor_scalar(comb[:, t, :], rl_scr[:, 0:8], lt_scr[:, 3:4], None, ALU.mult))
            vch.wait()

    def emit_p5(tT_v, ACC_v, ws, act_tr, actT_scr, tTs_p, junk5):
        wgh, wuh, wdh = ws
        s_gu, s_dn = g("s_qk"), g("s_pv")
        swD = g("s_wgD")
        swA_p = (g("s_wgA"), g("s_wA1"))
        swB_p = (g("s_wgB"), g("s_wB1"))
        bgu0, bdn0 = s_gu.n, s_dn.n

        def load_gu(p):
            if p >= 16:
                return
            par = p % 2
            e, ih = divmod(p, 2)
            if p >= 2:
                GP.wait_ge(s_gu.h, bgu0 + S * (p - 1))
            swA_p[par].inc(GP.dma_start(out=wgh[:, par, :, :],
                                 in_=wg_d[e, :, 512 * ih:512 * (ih + 1)].rearrange("(c p) f -> p c f", p=128)), 16)
            swB_p[par].inc(GP.dma_start(out=wuh[:, par, :, :],
                                 in_=wu_d[e, :, 512 * ih:512 * (ih + 1)].rearrange("(c p) f -> p c f", p=128)), 16)

        load_gu(0)
        load_gu(1)
        for e in range(8):
            for ih in range(2):
                p = 2 * e + ih
                par = p % 2
                if p >= 1:
                    GP.wait_ge(s_dn.h, bdn0 + S * p)
                swD.inc(GP.dma_start(out=wdh[:, :, :],
                                     in_=wd_d[e, 512 * ih:512 * (ih + 1), :].rearrange("(c p) f -> p c f", p=128)), 16)
                load_gu(p + 2)
                bgu, bdn = s_gu.n, s_dn.n
                bmk, bx2 = g("s_mk").n, g("s_x2").n
                bTa, bTevA, bpe = g("s_Ta").n, g("s_TevA").n, g("s_pe").n
                vch.wait()
                vch.pute(V.tensor_copy(tTs_p[0][:, :, :].rearrange("p c q -> p (c q)"),
                                       tT_v[:, 0, :, :].rearrange("p c q -> p (c q)")),
                         g("s_x2"))
                TE.wait_ge(swA_p[par].h, 16 * (p // 2 + 1))
                TE.wait_ge(swB_p[par].h, 16 * (p // 2 + 1))
                swDn = swD.n
                with reg_reclaim(), TE.Fori(0, S // 2) as m:
                    m2 = m * 2
                    for k in range(2):
                        TE.wait_ge(g("s_x2").h, bx2 + 1 + k + m2)
                        TE.wait_ge(g("s_mk").h, bmk - 1 + k + m2)
                        last = None
                        for c in range(16):
                            TE.ldweights(tTs_p[k][:, c, :])
                            TE.matmul(PSA[:, 0, 512 * k:512 * (k + 1)], tTs_p[k][:, c, :],
                                      wgh[:, par, c, :],
                                      start=(c == 0), stop=False, skip_group_check=True)
                            last = TE.matmul(PSA[:, 1, 512 * k:512 * (k + 1)], tTs_p[k][:, c, :],
                                             wuh[:, par, c, :],
                                             start=(c == 0), stop=False, skip_group_check=True)
                        last.then_inc(s_gu.h, 1)
                    for k in range(2):
                        TE.wait_ge(g("s_mk").h, bmk + 1 + k + m2)
                        TE.wait_ge(g("s_pe").h, bpe + 2 * (k + m2))
                        lt2 = None
                        for c4 in range(4):
                            lt2 = TE.transpose(Tps[:, 128 * c4:128 * (c4 + 1)],
                                               act_tr[:, 128 * c4:128 * (c4 + 1)], ident[:])
                        lt2.then_inc(g("s_Ta").h, 1)
                        TE.wait_ge(g("s_TevA").h, bTevA + 1 + k + m2)
                        TE.wait_ge(swD.h, swDn)
                        last = None
                        for c4 in range(4):
                            TE.ldweights(actT_scr[:, c4, :])
                            for bb in range(4):
                                dst = PSB[:, 512 * bb:512 * (bb + 1)] if bb < 2 else \
                                    PSC[:, 512 * (bb - 2):512 * (bb - 1)]
                                last = TE.matmul(dst, actT_scr[:, c4, :],
                                                 wdh[:, c4, 512 * bb:512 * (bb + 1)],
                                                 start=(c4 == 0), stop=False, skip_group_check=True)
                        last.then_inc(s_dn.h, 1)
                s_gu.n = bgu + S
                s_dn.n = bdn + S
                g("s_Ta").n = bTa + S
                def spill(t):
                    V.wait_ge(s_dn.h, bdn + t + 1)
                    vch.wait()
                    vch.put(V.tensor_add(ACC_v[:, t, 0:1024], ACC_v[:, t, 0:1024], PSB[:, :]))
                    vch.pute(V.tensor_add(ACC_v[:, t, 1024:2048], ACC_v[:, t, 1024:2048], PSC[:, :]),
                             g("s_pe"), 2)

                for t in range(S):
                    tp = t % 2
                    SC.wait_ge(s_gu.h, bgu + t + 1)
                    if t >= 1:
                        SC.wait_ge(g("s_mk").h, bmk + t)
                    sch.wait()
                    sch.pute(SC.activation(junk5[:, 0:512], PSA[:, 0, 512 * tp:512 * (tp + 1)],
                                           AF.Silu), g("s_hop"))
                    g("s_hop").wait(V)
                    vch.wait()
                    if t >= 1:
                        V.wait_ge(g("s_Ta").h, bTa + t)
                    vch.pute(V.scalar_tensor_tensor(act_tr[:, :], PSA[:, 1, 512 * tp:512 * (tp + 1)],
                                                    comb[:, t, e:e + 1], junk5[:, 0:512],
                                                    ALU.mult, ALU.mult), g("s_mk"))
                    if t + 1 < S:
                        vch.wait()
                        vch.pute(V.tensor_copy(tTs_p[(t + 1) % 2][:, :, :].rearrange("p c q -> p (c q)"),
                                               tT_v[:, t + 1, :, :].rearrange("p c q -> p (c q)")),
                                 g("s_x2"))
                    if t >= 1:
                        spill(t - 1)
                    SC.wait_ge(g("s_Ta").h, bTa + t + 1)
                    if t >= 1:
                        SC.wait_ge(s_dn.h, bdn + t)
                    sch.wait()
                    sch.pute(SC.copy(actT_scr[:, 0:4, :].rearrange("p c q -> p (c q)"),
                                                    Tps[:, 0:512]), g("s_TevA"))
                spill(S - 1)

    def emit_p5f(ACC_v, xb, outs):
        xld = []
        out_done = []
        sop = (g("s_out0"), g("s_out1"))

        def load(t):
            par = t % 2
            sx = (g("s_x0"), g("s_x1"))[par]
            sx.inc(SY.dma_start(out=xb[:, par, :], in_=x_own[128 * t:128 * (t + 1), :]), 16)
            xld.append(sx.n)

        load(0)
        if S > 1:
            load(1)
        for t in range(S):
            par = t % 2
            (g("s_x0"), g("s_x1"))[par].wait(V, xld[t])
            if t >= 2:
                sop[par].wait(V, out_done[t - 2])
            vch.wait()
            vch.pute(V.tensor_add(outs[:, par, :], ACC_v[:, t, :], xb[:, par, :]), g("s_r2"))
            g("s_r2").wait(SY)
            sop[par].inc(SY.dma_start(out=out_d[128 * t:128 * (t + 1), :], in_=outs[:, par, :]), 16)
            out_done.append(sop[par].n)
            if t + 2 < S:
                load(t + 2)
        SY.wait_ge(sop[0].h, sop[0].n)
        SY.wait_ge(sop[1].h, sop[1].n)

    kv_dram = nc.dram_tensor("kv_scratch", [T, 128], BF16).ap()
    kT1_dram = nc.dram_tensor("kT1_scratch", [128, NT, 2, 128], BF16).ap()
    v1_dram = nc.dram_tensor("v1_scratch", [128, NT, 2, 128], BF16).ap()
    rw_sb = sbuf("rw_sb", [128, 16, 8], BF16)

    xb = AR.alloc([128, 2, D], F32)
    xnscr = AR.alloc([128, 2, D], BF16)
    junkL = AR.alloc([128, D], BF16)
    qfT = AR.alloc([128, S, 16, 128], BF16)
    wqp = AR.alloc([128, 16, 2048], BF16, hi=True)
    wstg = AR.alloc([128, 2, 2048], F32, hi=True)
    qnwbc_a = AR.alloc([128, D], F32, hi=True)
    q_f = AR.alloc([128, D], F32, hi=True)
    q_sw = AR.alloc([128, 16, 128], F32, hi=True)
    qr1 = AR.alloc([128, 16, 128], F32, hi=True)
    q_bf = AR.alloc([128, D], BF16, hi=True)
    rqb = AR.alloc([128, 2, 2, 128], F32, hi=True)
    s_c0.inc(SY.dma_start(out=qnwbc_a[:, :], in_=qnorm_d.unsqueeze(0).partition_broadcast(128)), 16)
    fold_done = []
    wload = []
    swp = (g("s_w0"), g("s_w1"))
    s_wf = g("s_wf")

    def wq_load(c):
        if c >= 2:
            s_wf.wait(SY, fold_done[c - 2])
        swp[c % 2].inc(SY.dma_start(out=wstg[:, c % 2, :], in_=wq_d[128 * c:128 * (c + 1), :]), 16)
        wload.append(swp[c % 2].n)

    wq_load(0); wq_load(1)
    for c in range(16):
        swp[c % 2].wait(V, wload[c])
        vch.wait()
        vch.pute(V.tensor_scalar(wqp[:, c, :], wstg[:, c % 2, :], ln1_sb[:, c:c + 1], None, ALU.mult),
                 s_wf)
        fold_done.append(s_wf.n)
        if c + 2 < 16:
            wq_load(c + 2)
    s_c0.wait(V)
    g("s_cg").wait(V)
    g("s_cg").wait(V)
    if PROBE: PROBE("pre-P1", nc)
    emit_qbuild("ffn", wqp, qfT, xb, xnscr, q_f, q_sw, qr1, q_bf, rqb, junkL, qnwbc=qnwbc_a)
    if PROBE: PROBE("post-P1", nc)
    nc.all_engine_barrier()
    dump("qfT", qfT[:, :, :, :].rearrange("p s h q -> p (s h q)"), [128, S * 16 * 128], BF16)
    for b_ in (wqp, wstg, qnwbc_a, q_f, q_sw, qr1, q_bf, rqb):
        AR.free(b_)

    def p3_weights():
        wlat = AR.alloc([128, 16, 128], BF16, hi=True)
        wkv = AR.alloc([128, 16, 1024], BF16, hi=True)
        knwbc = AR.alloc([128, KW], F32, hi=True)
        wstg2 = AR.alloc([128, 2, 1024], F32, hi=True)
        g("s_cg").inc(GP.dma_start(out=wlat[:, :, :], in_=W_latent_d.rearrange("(c p) f -> p c f", p=128)), 16)
        s_c0.inc(SY.dma_start(out=knwbc[:, :], in_=knorm_d.unsqueeze(0).partition_broadcast(128)), 16)
        fold2, wload2 = [], []
        swp = (g("s_w0"), g("s_w1"))

        def kv_load(c):
            if c >= 2:
                s_wf.wait(SY, fold2[c - 2])
            swp[c % 2].inc(SY.dma_start(out=wstg2[:, c % 2, 0:512], in_=wk_d[128 * c:128 * (c + 1), :]), 16)
            swp[c % 2].inc(SY.dma_start(out=wstg2[:, c % 2, 512:1024], in_=wv_d[128 * c:128 * (c + 1), :]), 16)
            wload2.append(swp[c % 2].n)

        kv_load(0); kv_load(1)
        for c in range(16):
            swp[c % 2].wait(V, wload2[c])
            vch.wait()
            vch.pute(V.tensor_scalar(wkv[:, c, :], wstg2[:, c % 2, :], ln1_sb[:, c:c + 1], None, ALU.mult),
                     s_wf)
            fold2.append(s_wf.n)
            if c + 2 < 16:
                kv_load(c + 2)
        return wlat, wkv, knwbc, wstg2

    kTo_a = AR.alloc([128, 2, 256], BF16, hi=True)
    vo_a = AR.alloc([128, 2, 256], BF16, hi=True)
    kf32 = AR.alloc([128, 2, KW], F32, hi=True)
    ksw = AR.alloc([128, 4, 64], F32, hi=True)
    rop1 = AR.alloc([128, 4, 64], F32, hi=True)
    rop2 = AR.alloc([128, 4, 64], F32, hi=True)
    knb = AR.alloc([128, 2, KW], BF16, hi=True)
    rfb = AR.alloc([128, 2, 2, 64], F32, hi=True)
    junkK = AR.alloc([128, 2, KW], BF16, hi=True)
    wlat, wkv, knwbc, wstg2 = p3_weights()
    s_c0.wait(V)
    g("s_cg").wait(V)
    kT_g0 = AR.alloc([128, 2, NT, 128], BF16)
    v_g0 = AR.alloc([128, NT, 2, 128], BF16)
    kT_g0v, v_g0v = kT_g0, v_g0
    emit_p3_full(wlat, wkv, knwbc, kT_g0, v_g0, kvsc, xb, xnscr,
                 kf32, ksw, rop1, rop2, knb, rfb, junkL, junkK, kTo_a, vo_a)
    if PROBE: PROBE("post-P3", nc)
    nc.all_engine_barrier()
    dump("kT0", kT_g0[:, :, :, :].rearrange("p a t q -> p (a t q)"), [128, 2 * NT * 128], BF16)
    dump("v0", v_g0[:, :, :, :].rearrange("p t a d -> p (t a d)"), [128, 2 * NT * 128], BF16)
    emit_B_assembly("ffn")
    nc.all_engine_barrier()
    for b_ in (wlat, wkv, knwbc, wstg2, kf32, ksw, rop1, rop2, knb, rfb, junkK,
               kTo_a, vo_a):
        AR.free(b_)

    masksA = AR.alloc([128, NSFX, 1024], BF16)
    pbuf = AR.alloc([128, 2, 1024], BF16, hi=True)
    kstg = AR.alloc([128, 4, 2, 128], BF16, hi=True)
    vstg = AR.alloc([128, 4, 2, 128], BF16, hi=True)
    lrow0 = AR.alloc([1, 1024], F32, hi=True)
    OscF = AR.alloc([128, S, 16, 128], BF16)
    obuf_a = AR.alloc([128, 2048], BF16, hi=True)
    obuf = obuf_a
    masksv = masksA
    g("s_ks0").wait(SY); g("s_ks1").wait(SY)
    g("s_vs0").wait(SY); g("s_vs1").wait(SY)
    s_c0.inc(SY.dma_start(out=masksA[:, :, :], in_=masks_d), 16)
    s_c0.wait(V); s_c0.wait(TE); s_c0.wait(SC)
    for g2 in range(2):
        for t in range(S):
            Et = C * (t + 1)
            if PROBE: PROBE(f"pre-pass-{g2}-{t}", nc)
            emit_attn_pass("ffn", t, g2, 0, Et, Et, qfT, None)
            emit_dance(t, g2, True, OscF)
        if g2 == 0:
            nc.all_engine_barrier()
            AR.free(kT_g0)
            AR.free(v_g0)
    nc.all_engine_barrier()
    AR.free(qfT)
    for b_ in (pbuf, kstg, vstg, lrow0, obuf_a):
        AR.free(b_)
    dump("OscF", OscF[:, :, :, :].rearrange("p s h q -> p (s h q)"), [128, S * 16 * 128], BF16)

    x1_a = AR.alloc([128, S, D], BF16)
    q_fv = AR.alloc([128, D], F32, hi=True)
    wo_sb = AR.alloc([128, 16, 2048], BF16, hi=True)
    g("s_cg").inc(GP.dma_start(out=wo_sb[:, :, :], in_=wo_d.rearrange("(c p) f -> p c f", p=128)), 16)
    s_c0.wait(TE)
    g("s_cg").wait(TE)
    emit_poproj("o", OscF, wo_sb, x1_a, None, xb, junkL)
    nc.all_engine_barrier()
    AR.free(wo_sb)
    AR.free(OscF)
    AR.free(q_fv)
    dump("x1", x1_a[:, :, :].rearrange("p s d -> p (s d)"), [128, S * D], BF16)

    WdH = AR.alloc([128, 16, 1024], BF16, hi=True)
    qaT = AR.alloc([128, S, 16, 128], BF16)
    brow_d = AR.alloc([1, D], BF16, hi=True)
    g("s_cg").inc(GP.dma_start(out=brow_d[0:1, :], in_=b_d_d.unsqueeze(0)), 16)
    q_f2 = AR.alloc([128, 1024], F32, hi=True)
    q_sw2 = AR.alloc([128, 8, 128], F32, hi=True)
    qr12 = AR.alloc([128, 8, 128], F32, hi=True)
    q_bf2 = AR.alloc([128, 1024], BF16, hi=True)
    rqb2 = AR.alloc([128, 2, 2, 128], F32, hi=True)
    for hh in range(2):
        g("s_cg").inc(GP.dma_start(out=WdH[:, :, :],
                              in_=W_d_d[:, 1024 * hh:1024 * (hh + 1)].rearrange("(c p) f -> p c f", p=128)), 16)
        s_c0.wait(TE)
        g("s_cg").wait(TE)
        emit_qbuild("attn", WdH, qaT, xb, xnscr, q_f2, q_sw2, qr12, q_bf2, rqb2, junkL,
                    h0=8 * hh, nh=8)
        nc.all_engine_barrier()
    emit_B_assembly("attn")
    nc.all_engine_barrier()
    for b_ in (WdH, q_f2, q_sw2, qr12, q_bf2, rqb2, brow_d, xnscr):
        AR.free(b_)
    dump("qaT", qaT[:, :, :, :].rearrange("p s h q -> p (s h q)"), [128, S * 16 * 128], BF16)
    OscA = AR.alloc([128, S, 16, 128], BF16)

    kv_rawA = AR.alloc([128, NT, 128], BF16)
    kvTA = AR.alloc([128, NT, 128], BF16)
    kv_rawv, kvTv = kv_rawA, kvTA
    rib = AR.alloc([128, 2, 2, 128], F32, hi=True)
    rsw = AR.alloc([128, 128], F32, hi=True)
    rr1 = AR.alloc([128, 128], F32, hi=True)
    rr2 = AR.alloc([128, 128], F32, hi=True)
    rkb = AR.alloc([128, 128], BF16, hi=True)
    g("s_kvd0").wait(SY)
    g("s_kvd1").wait(SY)
    s_c0.inc(SY.dma_start(out=kv_rawA[:, :, :], in_=kv_dram.rearrange("(t p) d -> p t d", p=128)), 16)
    s_c0.wait(V)
    g("s_cg").wait(V); s_c0.wait(TE)
    emit_pr(kv_rawA, kvTA, rib, rsw, rr1, rr2, rkb)
    nc.all_engine_barrier()
    for b_ in (rib, rsw, rr1, rr2, rkb):
        AR.free(b_)
    dump("kvT", kvTA[:, :, :].rearrange("p t d -> p (t d)"), [128, NT * 128], BF16)

    pbuf = AR.alloc([128, 2, 1024], BF16, hi=True)
    kstg = AR.alloc([128, 4, 2, 128], BF16, hi=True)
    vstg = AR.alloc([128, 4, 2, 128], BF16, hi=True)
    lrow0 = AR.alloc([1, 1024], F32, hi=True)
    obuf_a = AR.alloc([128, 2048], BF16, hi=True)
    obuf = obuf_a
    for t in range(S):
        Et = C * (t + 1)
        for g2 in range(2):
            emit_attn_pass("attn", t, g2, 0, Et, Et, qaT, None)
            emit_dance(t, g2, True, OscA)
    nc.all_engine_barrier()
    for b_ in (kv_rawA, kvTA, qaT, masksA, obuf_a, pbuf, kstg, vstg, lrow0):
        AR.free(b_)
    dump("OscA", OscA[:, :, :, :].rearrange("p s h q -> p (s h q)"), [128, S * 16 * 128], BF16)

    ACC = AR.alloc([128, S, D], F32)
    ACC_v = ACC
    WpH = AR.alloc([128, 16, 1024], BF16, hi=True)
    brow_proj = AR.alloc([1, D], BF16, hi=True)
    g("s_cg").inc(GP.dma_start(out=brow_proj[0:1, :], in_=b_proj_d.unsqueeze(0)), 16)
    for hh in range(2):
        g("s_cg").inc(GP.dma_start(out=WpH[:, :, :],
                              in_=W_proj_d[:, 1024 * hh:1024 * (hh + 1)].rearrange("(c p) f -> p c f", p=128)), 16)
        s_c0.wait(TE)
        g("s_cg").wait(TE)
        emit_poproj("out" + str(hh), OscA, WpH, x1_a, ACC, xb, junkL)
        nc.all_engine_barrier()
    AR.free(WpH)
    AR.free(OscA)
    AR.free(brow_proj)
    dump("ACC1", ACC[:, :, :].rearrange("p s d -> p (s d)"), [128, S * D], F32)

    tT = AR.alloc([128, S, 16, 128], BF16)
    ln2bc = AR.alloc([128, D], F32, hi=True)
    s_c0.inc(SY.dma_start(out=ln2bc[:, :], in_=ln2_d.unsqueeze(0).partition_broadcast(128)), 16)
    g("s_cg").inc(GP.dma_start(out=rw_sb[:, :, :], in_=router_w_d.rearrange("(c p) f -> p c f", p=128)), 16)
    s_c0.wait(V)
    g("s_cg").wait(V); s_c0.wait(TE); s_c0.wait(SC)
    g("s_cg").wait(TE)
    emit_p5pre(x1_a, tT, ln2bc, junkL, ot_stage)
    nc.all_engine_barrier()
    AR.free(x1_a)
    AR.free(ln2bc)
    dump("comb", comb[:, :, :].rearrange("p s e -> p (s e)"), [128, S * 8], F32)
    dump("tT", tT[:, :, :, :].rearrange("p s c q -> p (s c q)"), [128, S * 16 * 128], BF16)

    AR.free(xb)
    wgh_a = AR.alloc([128, 2, 16, 512], BF16, hi=True)
    wuh_a = AR.alloc([128, 2, 16, 512], BF16, hi=True)
    wdh_a = AR.alloc([128, 4, 2048], BF16, hi=True)
    p5tT2 = AR.alloc([128, 16, 128], BF16)
    emit_p5(tT, ACC, (wgh_a, wuh_a, wdh_a), ot_stage[:, 0:512], otT_scr, (xnT_scr, p5tT2),
            junkL[:, 0:1024])
    nc.all_engine_barrier()
    for b_ in (wgh_a, wuh_a, wdh_a, tT, junkL, p5tT2):
        AR.free(b_)

    xb = AR.alloc([128, 2, D], F32)
    out_scr = AR.alloc([128, 2, D], F32, hi=True)
    emit_p5f(ACC, xb, out_scr)
    SY.wait_ge(g("s_dbg").h, g("s_dbg").n)
    es.close()
    return nc, dbg_aps


def _inputs_for_core(cfg, core, consts, W):
    own = own_row_index(cfg, core)
    d = dict(consts)
    d["x_full"] = W["x"]
    d["x_own"] = np.ascontiguousarray(W["x"][own])
    d.update({k: W[k] for k in ("W_latent b_latent W_d b_d W_proj b_proj ln1_w ln2_w "
                                "wq wk wv wo qnorm_w knorm_w router_w router_bias "
                                "wg wu wd").split()})
    return d


def kernel(**inputs):
    cfg = Cfg(NT=64)
    x = np.ascontiguousarray(np.asarray(inputs["L0_attn_x"], np.float32).reshape(cfg.T, D))
    W = dict(
        x=x,
        W_latent=np.asarray(inputs["L0_attn_W_latent"], np.float32),
        b_latent=np.asarray(inputs["L0_attn_b_latent"], np.float32),
        W_d=np.asarray(inputs["L0_attn_W_d"], np.float32),
        b_d=np.asarray(inputs["L0_attn_b_d"], np.float32),
        W_proj=np.asarray(inputs["L0_attn_W_proj"], np.float32),
        b_proj=np.asarray(inputs["L0_attn_b_proj"], np.float32),
        ln1_w=np.asarray(inputs["L0_ffn_ln1_w"], np.float32),
        ln2_w=np.asarray(inputs["L0_ffn_ln2_w"], np.float32),
        wq=np.asarray(inputs["L0_ffn_wq"], np.float32),
        wk=np.asarray(inputs["L0_ffn_wk"], np.float32),
        wv=np.asarray(inputs["L0_ffn_wv"], np.float32),
        wo=np.asarray(inputs["L0_ffn_wo"], np.float32),
        qnorm_w=np.asarray(inputs["L0_ffn_qnorm_w"], np.float32),
        knorm_w=np.asarray(inputs["L0_ffn_knorm_w"], np.float32),
        router_w=np.asarray(inputs["L0_ffn_router_w"], np.float32),
        router_bias=np.asarray(inputs["L0_ffn_router_bias"], np.float32),
        wg=np.asarray(inputs["L0_ffn_wg"], np.float32),
        wu=np.asarray(inputs["L0_ffn_wu"], np.float32),
        wd=np.asarray(inputs["L0_ffn_wd"], np.float32),
    )
    nc, _ = build_program(cfg)
    in_maps = [_inputs_for_core(cfg, i, host_consts(cfg, i), W) for i in range(C)]
    res = run_bass_kernel_spmd(nc, in_maps, core_ids=list(range(C)))
    out = np.empty((cfg.T, D), np.float32)
    for i in range(C):
        out[own_row_index(cfg, i)] = res.results[i]["out_own"]
    return out.reshape(1, cfg.T, D)


# version 09
# speedup vs baseline: 1.0698x; 1.0698x over previous
import math
from contextlib import ExitStack

import numpy as np
import ml_dtypes

import concourse.bass as bass
import concourse.mybir as mybir
from concourse.bass_utils import run_bass_kernel_spmd

F32 = mybir.dt.float32
BF16 = mybir.dt.bfloat16
AF = mybir.ActivationFunctionType
ALU = mybir.AluOpType
AX = mybir.AxisListType
BF = ml_dtypes.bfloat16

PROBE = None

C = 8
D = 2048
KW = 512
LCOMP = 128
NSFX = 8
ISQ = float(1.0 / math.sqrt(128.0))


class Cfg:
    def __init__(self, NT=64, debug=()):
        self.NT = NT
        self.S = NT // C
        self.HT = NT // 2
        self.T = 128 * NT
        self.debug = set(debug)


def own_row_index(cfg, core):
    idx = []
    for t in range(cfg.S):
        g = C * t + core
        idx.extend(range(128 * g, 128 * g + 128))
    return np.array(idx)


def host_consts(cfg, core):
    T = cfg.T
    pos = np.arange(T, dtype=np.float64)
    inv = 10000.0 ** (-np.arange(64, dtype=np.float64) / 64.0)
    ang = pos[:, None] * inv[None, :]
    cosI = np.repeat(np.cos(ang), 2, axis=1).astype(np.float32)
    sinI = np.repeat(np.sin(ang), 2, axis=1).astype(np.float32)
    sinI[:, 0::2] *= -1.0
    invf = 1.0 / (1000000.0 ** (np.arange(0, 64, 2, dtype=np.float64) / 64.0))
    angf = pos[:, None] * invf[None, :]
    cosF = np.concatenate([np.cos(angf), np.cos(angf)], axis=1).astype(np.float32)
    sinF = np.concatenate([np.sin(angf), np.sin(angf)], axis=1).astype(np.float32)
    sinF[:, 0:32] *= -1.0
    own = own_row_index(cfg, core)
    kk = np.arange(128)
    tri = (kk[:, None] <= kk[None, :]).astype(np.float32)
    m = np.zeros((128, NSFX, 8, 128), np.float32)
    for j in range(NSFX):
        if j < core:
            m[:, j] = 1.0
        elif j == core:
            m[:, j] = tri[:, None, :]
    return dict(cosI=cosI, sinI=sinI, cosF=cosF, sinF=sinF,
                cosIq=np.ascontiguousarray(cosI[own]), sinIq=np.ascontiguousarray(sinI[own]),
                cosFq=np.ascontiguousarray(cosF[own]), sinFq=np.ascontiguousarray(sinF[own]),
                masks=m.reshape(128, NSFX, 1024).astype(BF),
                ident=np.eye(128, dtype=BF))


class Sem:
    def __init__(self, es, nc, name):
        self.h = es.enter_context(nc.semaphore(name))
        self.n = 0

    def inc(self, instr, k=1):
        instr.then_inc(self.h, k)
        self.n += k
        return instr

    def wait(self, eng, val=None):
        eng.wait_ge(self.h, self.n if val is None else val)


_EMITTING_BLOCKLIST = {"wait_ge", "sem_inc", "wait_op", "maybe_wait_ge", "Fori",
                       "register", "register64", "reg_mov", "reg_add", "reg_save"}


class _Rec:
    __slots__ = ("instr", "mark")

    def __init__(self, instr):
        self.instr = instr
        self.mark = None


class EngWrap:

    def __init__(self, es, nc, raw, name):
        self.raw = raw
        self.own = Sem(es, nc, name)
        self.q = []
        self.lb = 1

    def _mark(self, rec):
        if rec.mark is None:
            self.own.inc(rec.instr)
            rec.mark = (self.own.h, self.own.n)
        return rec.mark

    def __getattr__(self, name):
        fn = getattr(self.raw, name)
        if name in _EMITTING_BLOCKLIST or not callable(fn):
            return fn

        def wrapped(*a, **kw):
            if len(self.q) >= self.lb:
                mark = self._mark(self.q[-self.lb])
                self.raw.wait_ge(mark[0], mark[1])
                del self.q[:len(self.q) - self.lb + 1]
            instr = fn(*a, **kw)
            self.q.append(_Rec(instr))
            return instr

        return wrapped

    @property
    def pending(self):
        return self.q[-1].instr if self.q else None

    def claim(self, sem, k=1):
        assert self.q
        rec = self.q[-1]
        assert rec.mark is None
        sem.inc(rec.instr, k)
        rec.mark = (sem.h, sem.n)
        return rec.instr

    def wait(self):
        if self.q:
            mark = self._mark(self.q[-1])
            self.raw.wait_ge(mark[0], mark[1])
            self.q.clear()


class Chain:

    def __init__(self, wrap):
        self.wrap = wrap

    def put(self, instr):
        return instr

    def pute(self, instr, sem, k=1):
        return self.wrap.claim(sem, k)

    def wait(self):
        self.wrap.wait()


class Arena:

    def __init__(self, nc, es, kb=198):
        self.nbytes = kb * 1024
        self.t = es.enter_context(nc.sbuf_tensor("ARENA", [128, self.nbytes // 2], BF16))
        self.free_list = [(0, self.nbytes)]

    def alloc(self, shape, dt, hi=False):
        per = int(np.prod(shape[1:])) * (2 if dt == BF16 else 4)
        per = (per + 63) & ~63
        if hi:
            for i in range(len(self.free_list) - 1, -1, -1):
                off, size = self.free_list[i]
                if size >= per:
                    self.free_list[i] = (off, size - per)
                    if self.free_list[i][1] == 0:
                        self.free_list.pop(i)
                    return ArenaView(self, off + size - per, per, shape, dt)
        for i, (off, size) in enumerate(self.free_list):
            if size >= per:
                self.free_list[i] = (off + per, size - per)
                if self.free_list[i][1] == 0:
                    self.free_list.pop(i)
                return ArenaView(self, off, per, shape, dt)
        raise MemoryError(f"arena full: want {per}B, free={self.free_list}")

    def free(self, view):
        self.free_list.append((view.off, view.per))
        self.free_list.sort()
        merged = []
        for off, size in self.free_list:
            if merged and merged[-1][0] + merged[-1][1] == off:
                merged[-1] = (merged[-1][0], merged[-1][1] + size)
            else:
                merged.append((off, size))
        self.free_list = [tuple(x) for x in merged]


class ArenaView:
    def __init__(self, arena, off, per, shape, dt):
        self.arena, self.off, self.per, self.shape, self.dt = arena, off, per, shape, dt
        n = int(np.prod(shape[1:]))
        ap = arena.t[:, off // 2: off // 2 + n * (1 if dt == BF16 else 2)]
        if dt != BF16:
            ap = ap.bitcast(dt)
        if len(shape) > 2:
            dims = " ".join(f"d{i}" for i in range(1, len(shape)))
            kw = {f"d{i}": shape[i] for i in range(1, len(shape))}
            ap = ap.rearrange(f"p ({dims}) -> p {dims}", **kw)
        self.ap = ap

    def __getitem__(self, key):
        return self.ap[key]


def build_program(cfg):
    nc = bass.Bass("TRN2", target_bir_lowering=False)
    NT, S, HT = cfg.NT, cfg.S, cfg.HT
    T = cfg.T

    def din(name, shape, dt=F32):
        return nc.dram_tensor(name, shape, dt, kind="ExternalInput").ap()

    x_full = din("x_full", [T, D])
    x_own = din("x_own", [128 * S, D])
    cosI_d = din("cosI", [T, 128]); sinI_d = din("sinI", [T, 128])
    cosF_d = din("cosF", [T, 64]); sinF_d = din("sinF", [T, 64])
    cosIq_d = din("cosIq", [128 * S, 128]); sinIq_d = din("sinIq", [128 * S, 128])
    cosFq_d = din("cosFq", [128 * S, 64]); sinFq_d = din("sinFq", [128 * S, 64])
    masks_d = din("masks", [128, NSFX, 1024], BF16)
    ident_d = din("ident", [128, 128], BF16)
    W_latent_d = din("W_latent", [D, LCOMP]); b_latent_d = din("b_latent", [LCOMP])
    W_d_d = din("W_d", [D, D]); b_d_d = din("b_d", [D])
    W_proj_d = din("W_proj", [D, D]); b_proj_d = din("b_proj", [D])
    ln1_d = din("ln1_w", [D]); ln2_d = din("ln2_w", [D])
    wq_d = din("wq", [D, D]); wk_d = din("wk", [D, KW]); wv_d = din("wv", [D, KW])
    wo_d = din("wo", [D, D])
    qnorm_d = din("qnorm_w", [D]); knorm_d = din("knorm_w", [KW])
    router_w_d = din("router_w", [D, 8]); router_b_d = din("router_bias", [8])
    wg_d = din("wg", [8, D, 1024]); wu_d = din("wu", [8, D, 1024]); wd_d = din("wd", [8, 1024, D])

    out_d = nc.dram_tensor("out_own", [128 * S, D], F32, kind="ExternalOutput").ap()
    dbg_aps = {}

    es = ExitStack()
    AR = Arena(nc, es, kb=203)

    def sbuf(name, shape, dt):
        return es.enter_context(nc.sbuf_tensor(name, shape, dt))

    def sem(name):
        return Sem(es, nc, name)

    ident = sbuf("ident_s", [128, 128], BF16)
    ones_row = sbuf("ones_row", [1, 128], BF16)
    ones_col = sbuf("ones_col", [128, 1], BF16)
    ln1_sb = sbuf("ln1_sb", [128, 16], F32)
    brow_lat = sbuf("brow_lat", [1, LCOMP], BF16)
    biasB = sbuf("biasB", [128, 4], F32)
    stat = sbuf("stat", [128, 2, 16], F32)
    qssh = sbuf("qssh", [128, 2, 16], F32)
    kssh = sbuf("kssh", [128, 16], F32)
    ktmp4 = sbuf("ktmp4", [128, 8], F32)
    qtmp16 = sbuf("qtmp16", [128, 16], F32)
    brow_scr = sbuf("brow_scr", [1, 32], F32)
    lt_scr = sbuf("lt_scr", [128, 16], F32)
    rl_scr = sbuf("rl_scr", [128, 16], F32)
    comb = sbuf("comb", [128, S, 8], F32)
    rbias_bc = sbuf("rbias_bc", [128, 8], F32)
    kvsc = sbuf("kvsc", [128, 2, 128], BF16)
    r2s = sbuf("r2s", [128, S], F32)

    xnT_scr = AR.alloc([128, 16, 128], BF16)
    otT_scr = AR.alloc([128, 16, 128], BF16)
    ot_stage = AR.alloc([128, 2048], BF16)

    PSA = es.enter_context(nc.psum_tensor("PSA", [128, 2, 1024], F32))
    PSB = es.enter_context(nc.psum_tensor("PSB", [128, 1024], F32))
    PSC = es.enter_context(nc.psum_tensor("PSC", [128, 1024], F32))
    PSA_flat = PSA[:, :, :].rearrange("p a b -> p (a b)")

    names = ("s_c0 s_x s_ss s_hop s_hop2 s_xn s_Ta s_TevA s_pj s_pe s_knb s_Tb "
             "s_TevB s_w s_wf s_qk s_exp s_mk s_pv s_sp s_stg s_dnc s_gen s_out "
             "s_dbg s_kssv s_kvmax s_qb s_x2 s_r2 s_sg s_cg "
             "s_x0 s_x1 s_stg0 s_stg1 s_w0 s_w1 s_wgA s_wgB s_wgD s_kvd0 s_kvd1 "
             "s_out0 s_out1 s_ks0 s_ks1 s_vs0 s_vs1 s_ko s_vo s_stg2 s_stg3 s_wA1 s_wB1").split()
    SM = {n: sem(n) for n in names}
    g = lambda n: SM[n]
    V = EngWrap(es, nc, nc.vector, "vch")
    SC = EngWrap(es, nc, nc.scalar, "sch")
    vch, sch = Chain(V), Chain(SC)
    TE, SY, GP = nc.tensor, nc.sync, nc.gpsimd

    from contextlib import contextmanager

    @contextmanager
    def reg_reclaim():
        n0 = nc.n_instructions()
        yield
        from concourse.bass_primitives_rust import RegisterHandle
        emap = {"PE": mybir.EngineType.PE, "Pool": mybir.EngineType.Pool,
                "Activation": mybir.EngineType.Activation,
                "SP": mybir.EngineType.SP, "DVE": mybir.EngineType.DVE}
        seen = set()
        for nm in nc.inst_names()[n0:]:
            ii = nc.get_inst_by_name(nm)
            for arg in list(ii.ins) + list(ii.outs):
                r = getattr(arg, "regref", None) or getattr(arg, "name", None)
                if isinstance(r, str) and ("_tmp_" in r or "_fori_" in r):
                    seen.add(r)
        for r in seen:
            eng = emap.get(r.split("_", 1)[0])
            if eng is None:
                continue
            try:
                nc.free_register(RegisterHandle(name=r, engine=eng))
            except ValueError:
                pass

    def dump(name, src_ap, shape, dt=F32):
        if name not in cfg.debug:
            return
        dd = nc.dram_tensor("dbg_" + name, shape, dt, kind="ExternalOutput").ap()
        dbg_aps[name] = dd
        g("s_dbg").inc(SY.dma_start(out=dd, in_=src_ap), 16)
        g("s_dbg").wait(SY)

    s_c0 = g("s_c0")
    s_c0.inc(SY.dma_start(out=ident[:], in_=ident_d), 16)
    s_c0.inc(SY.dma_start(out=ln1_sb[:], in_=ln1_d.rearrange("(c p) -> p c", p=128), allow_slow_non_contiguous=True), 16)
    g("s_cg").inc(GP.dma_start(out=brow_lat[:], in_=b_latent_d.unsqueeze(0)), 16)
    s_c0.inc(SY.dma_start(out=rbias_bc[:], in_=router_b_d.unsqueeze(0).partition_broadcast(128)), 16)
    V.memset(ones_row[:], 1.0)
    V.memset(ones_col[:], 1.0)
    V.memset(qssh[:].rearrange("p a b -> p (a b)"), 0.0)
    V.memset(kssh[:], 0.0)
    s_c0.wait(V)
    g("s_cg").wait(V)
    nc.all_engine_barrier()

    def emit_ln(par, src_ap, want_rf1, xn_dst=None, junkL=None):
        st = lambda c: stat[:, par, c:c + 1]
        vch.put(V.reduce_sum(st(0), src_ap, axis=AX.X))
        sch.pute(SC.activation(junkL[:], src_ap, AF.Square, accum_out=st(1)), g("s_ss"))
        g("s_ss").wait(V)
        vch.wait()
        vch.put(V.tensor_scalar(st(2), st(0), 1.0 / D, None, ALU.mult))
        vch.put(V.tensor_scalar(st(3), st(1), 1.0 / D, None, ALU.mult))
        vch.wait()
        vch.put(V.scalar_tensor_tensor(st(4), st(2), st(2), st(3), ALU.mult, ALU.subtract))
        vch.wait()
        vch.put(V.tensor_scalar(st(4), st(4), -1.0, 1e-5, ALU.mult, ALU.add))
        vch.wait()
        vch.put(V.reciprocal(st(5), st(4)))
        vch.wait()
        if want_rf1:
            vch.put(V.tensor_scalar(st(8), st(5), -1e-5, 1.0 + 1e-6, ALU.mult, ALU.add))
            vch.wait()
            vch.put(V.reciprocal(st(9), st(8)))
            vch.wait()
        V.engine_nop()
        V.claim(g("s_hop"))
        g("s_hop").wait(SC)
        sch.wait()
        if want_rf1:
            sch.put(SC.activation(st(6), st(5), AF.Sqrt))
            sch.pute(SC.activation(st(10), st(9), AF.Sqrt), g("s_hop2"))
        else:
            sch.pute(SC.activation(st(6), st(5), AF.Sqrt), g("s_hop2"))
        g("s_hop2").wait(V)
        vch.put(V.tensor_scalar(st(7), st(2), st(6), -1.0, ALU.mult, ALU.mult))
        vch.wait()
        if xn_dst is not None:
            ins = V.tensor_scalar(xn_dst, src_ap, st(6), st(7), ALU.mult, ALU.add)
            vch.pute(ins, g("s_xn"))

    Tps = PSC[:, :].bitcast(BF16)
    ident32 = sbuf("ident32_s", [128, 128], F32)
    g("s_cg").inc(GP.dma_start(out=ident32[:], in_=ident_d), 16)
    tstat = sbuf("tstat", [16, 256], F32)
    negBbf = sbuf("negBbf", [1, 4], BF16)

    def transpose16(src2048_bf, n=16):
        last = None
        for c in range(n):
            last = TE.transpose(Tps[:, 128 * c:128 * (c + 1)],
                                src2048_bf[:, 128 * c:128 * (c + 1)], ident[:])
        return last

    def emit_B_assembly(branch):
        ix = 0 if branch == "ffn" else 1
        if branch == "ffn":
            vch.put(V.tensor_max(kssh[:, 0:5], kssh[:, 0:5], kssh[:, 8:13]))
            vch.wait()
        qcol = qssh[:, ix, :]
        nk = 4 if branch == "ffn" else 1
        kcol = kssh[:, 0:nk] if branch == "ffn" else kssh[:, 4:5]
        vch.wait(); sch.wait()
        V.engine_nop()
        V.claim(g("s_gen")); g("s_gen").wait(TE)
        TE.transpose(PSB[0:16, 0:128], qcol, ident32[:])
        i1 = TE.transpose(PSB[0:nk, 128:256], kcol, ident32[:])
        g("s_gen").inc(i1); g("s_gen").wait(V)
        vch.put(V.tensor_copy(tstat[:, 0:128], PSB[0:16, 0:128]))
        vch.put(V.tensor_copy(tstat[0:nk, 128:256], PSB[0:nk, 128:256]))
        vch.wait()
        vch.put(V.reduce_max(tstat[:, 140:141], tstat[:, 0:128], axis=AX.X))
        vch.put(V.reduce_max(tstat[0:nk, 142:143], tstat[0:nk, 128:256], axis=AX.X))
        vch.wait()
        V.engine_nop()
        V.claim(g("s_gen")); g("s_gen").wait(TE)
        TE.transpose(PSB[0:1, 0:16], tstat[:, 140:141], ident32[0:16, 0:16])
        i2 = TE.transpose(PSB[0:1, 16:16 + nk], tstat[0:nk, 142:143], ident32[0:nk, 0:nk])
        g("s_gen").inc(i2); g("s_gen").wait(V)
        vch.put(V.tensor_copy(brow_scr[0:1, 0:16], PSB[0:1, 0:16]))
        vch.put(V.tensor_copy(brow_scr[0:1, 16:16 + nk], PSB[0:1, 16:16 + nk]))
        vch.wait()
        if branch == "ffn":
            for a2 in range(4):
                vch.put(V.tensor_scalar(brow_scr[0:1, a2 * 4:a2 * 4 + 4], brow_scr[0:1, a2 * 4:a2 * 4 + 4],
                                        brow_scr[0:1, 16 + a2:17 + a2], None, ALU.mult))
        else:
            vch.put(V.tensor_scalar(brow_scr[0:1, 0:16], brow_scr[0:1, 0:16],
                                    brow_scr[0:1, 16:17], None, ALU.mult))
        vch.wait()
        V.engine_nop()
        V.claim(g("s_hop")); g("s_hop").wait(SC)
        sch.wait()
        sch.pute(SC.activation(brow_scr[0:1, 0:16], brow_scr[0:1, 0:16], AF.Sqrt), g("s_hop2"))
        g("s_hop2").wait(V)
        vch.put(V.tensor_scalar(brow_scr[0:1, 0:16], brow_scr[0:1, 0:16], -ISQ, None, ALU.mult))
        vch.wait()
        for gg in range(2):
            vch.put(V.tensor_reduce(brow_scr[0:1, 20 + gg:21 + gg], brow_scr[0:1, 8 * gg:8 * gg + 8],
                                    AX.X, ALU.min))
        vch.wait()
        vch.put(V.tensor_copy(negBbf[0:1, 2 * ix:2 * ix + 2], brow_scr[0:1, 20:22]))
        vch.wait()
        V.engine_nop()
        V.claim(g("s_gen")); g("s_gen").wait(TE)
        TE.ldweights(ones_row[0:1, :])
        i3 = None
        for gg in range(2):
            i3 = TE.matmul(PSB[:, 32 + gg:33 + gg], ones_row[0:1, :], negBbf[0:1, 2 * ix + gg:2 * ix + gg + 1],
                           start=True, stop=True, skip_group_check=True)
        g("s_gen").inc(i3); g("s_gen").wait(V)
        vch.put(V.tensor_copy(biasB[:, 2 * ix:2 * ix + 2], PSB[:, 32:34]))
        vch.wait()

    def emit_qbuild(branch, wbig, qT_arr, xb, xnscr, q_f, q_sw, qr1, q_bf,
                    rq_buf, junkL, qnwbc=None, h0=0, nh=16):
        nc_cols = nh * 128
        nch = nc_cols // 512
        ffn = branch == "ffn"
        sxp = (g("s_x0"), g("s_x1"))
        s_xn, s_ss = g("s_xn"), g("s_ss")
        ln_done = []
        qred_done = []
        rope_done = []
        ta_done = []
        pe_done = []
        tb_done = []

        def load(t):
            par = t % 2
            sx = sxp[par]
            if t >= 2:
                s_xn.wait(SY, ln_done[t - 2][0])
                s_ss.wait(SY, ln_done[t - 2][1])
                g("s_qb").wait(SY, rope_done[t - 2])
            sx.inc(SY.dma_start(out=xb[:, par, :], in_=x_own[128 * t:128 * (t + 1), :]), 16)
            if ffn:
                sx.inc(SY.dma_start(out=rq_buf[:, par, 0, 0:64], in_=cosFq_d[128 * t:128 * (t + 1), :]), 16)
                sx.inc(SY.dma_start(out=rq_buf[:, par, 1, 0:64], in_=sinFq_d[128 * t:128 * (t + 1), :]), 16)
            else:
                sx.inc(SY.dma_start(out=rq_buf[:, par, 0, :], in_=cosIq_d[128 * t:128 * (t + 1), :]), 16)
                sx.inc(SY.dma_start(out=rq_buf[:, par, 1, :], in_=sinIq_d[128 * t:128 * (t + 1), :]), 16)
            xload_done.append(sx.n)

        xload_done = []
        load(0)
        if S > 1:
            load(1)
        for t in range(S):
            par = t % 2
            xslice = xb[:, par, :]
            sxp[par].wait(V, xload_done[t]); sxp[par].wait(SC, xload_done[t])
            if t >= 2:
                g("s_Ta").wait(V, ta_done[t - 2])
            if t >= 1:
                g("s_kssv").wait(SC, qred_done[t - 1])
            emit_ln(par, xslice, want_rf1=ffn, xn_dst=None, junkL=junkL)
            st = lambda c: stat[:, par, c:c + 1]
            if ffn:
                vch.put(V.tensor_scalar(st(6), st(6), st(10), None, ALU.mult))
                vch.put(V.tensor_scalar(st(7), st(7), st(10), None, ALU.mult))
                vch.wait()
            vch.pute(V.tensor_scalar(xnscr[:, par, :], xslice, st(6), st(7), ALU.mult, ALU.add), g("s_xn"))
            ln_done.append((s_xn.n, s_ss.n))
            g("s_xn").wait(TE)
            if t >= 1:
                g("s_TevB").wait(TE, tb_done[t - 1])
            g("s_Ta").inc(transpose16(xnscr[:, par, :]))
            ta_done.append(g("s_Ta").n)
            g("s_Ta").wait(SC)
            sch.wait()
            sch.pute(SC.copy(xnT_scr[:, :, :].rearrange("p c q -> p (c q)"), Tps[:, :]), g("s_TevA"))
            g("s_TevA").wait(TE)
            if t >= 1:
                g("s_pe").wait(TE, pe_done[t - 1])
            last = None
            for c in range(16):
                TE.ldweights(xnT_scr[:, c, :])
                for b in range(nch):
                    last = TE.matmul(PSA_flat[:, 512 * b:512 * (b + 1)], xnT_scr[:, c, :],
                                     wbig[:, c, 512 * b:512 * (b + 1)],
                                     start=(c == 0), stop=False, skip_group_check=True)
            if not ffn:
                TE.ldweights(ones_row[0:1, :])
                for b in range(nch):
                    last = TE.matmul(PSA_flat[:, 512 * b:512 * (b + 1)], ones_row[0:1, :],
                                     brow_d[0:1, 128 * h0 + 512 * b:128 * h0 + 512 * (b + 1)],
                                     start=False, stop=True, skip_group_check=True)
            g("s_pj").inc(last)
            g("s_pj").wait(SC)
            sch.wait()
            if ffn:
                sch.put(SC.activation(junkL[:, 0:nc_cols], PSA_flat[:, 0:nc_cols], AF.Square,
                                      accum_out=st(11)))
            sch.pute(SC.copy(q_f[:, 0:nc_cols], PSA_flat[:, 0:nc_cols]), g("s_pe"))
            pe_done.append(g("s_pe").n)
            g("s_pe").wait(V)
            if ffn:
                g("s_ss").wait(V)
                sch_count_hop = g("s_hop")
                vch.put(V.tensor_scalar(st(3), st(11), 1.0 / D, 1e-6, ALU.mult, ALU.add))
                vch.wait()
                vch.put(V.reciprocal(st(4), st(3)))
                vch.wait()
                V.engine_nop(); V.claim(sch_count_hop)
                sch_count_hop.wait(SC)
                sch.wait()
                sch.pute(SC.activation(st(11), st(4), AF.Sqrt), g("s_hop2"))
                g("s_hop2").wait(V)
                vch.put(V.scalar_tensor_tensor(q_f[:, 0:nc_cols], q_f[:, 0:nc_cols], st(11), qnwbc[:, :], ALU.mult, ALU.mult))
                vch.wait()
                qsrc = q_f
            else:
                qsrc = q_f
            V.engine_nop()
            V.claim(g("s_gen"))
            g("s_gen").wait(SC)
            sch.wait()
            sch.pute(SC.activation(junkL[:, 0:nc_cols], qsrc[:, 0:nc_cols], AF.Square), g("s_hop"))
            g("s_hop").wait(V)
            vch.put(V.tensor_reduce(qtmp16[:, h0:h0 + nh].rearrange("p (h o) -> p h o", o=1),
                                    junkL[:, 0:nc_cols].rearrange("p (h d) -> p h d", h=nh), AX.X, ALU.add))
            vch.wait()
            vch.put(V.tensor_max(qssh[:, 0 if ffn else 1, h0:h0 + nh],
                                 qssh[:, 0 if ffn else 1, h0:h0 + nh], qtmp16[:, h0:h0 + nh]))
            vch.pute(V.engine_nop(), g("s_kssv"))
            qred_done.append(g("s_kssv").n)
            q3 = qsrc[:, 0:nc_cols].rearrange("p (h d) -> p h d", h=nh)
            qb3 = q_bf[:, 0:nc_cols].rearrange("p (h d) -> p h d", h=nh)
            if ffn:
                rot = q3[:, :, 0:64]
                sw_src = rot.rearrange("p h (b c) -> p h b c", b=2)[:, :, ::-1, :]
                vch.put(V.tensor_copy(q_sw[:, 0:nh, 0:64].rearrange("p h (b c) -> p h b c", b=2), sw_src))
                cosb = rq_buf[:, par, 0, 0:64].unsqueeze(1).broadcast_to([128, nh, 64])
                sinb = rq_buf[:, par, 1, 0:64].unsqueeze(1).broadcast_to([128, nh, 64])
                vch.put(V.tensor_mul(qr1[:, 0:nh, 0:64], rot, cosb))
                vch.wait()
                vch.put(V.tensor_mul(q_sw[:, 0:nh, 0:64], q_sw[:, 0:nh, 0:64], sinb))
                vch.wait()
                vch.put(V.tensor_add(qb3[:, :, 0:64], qr1[:, 0:nh, 0:64], q_sw[:, 0:nh, 0:64]))
                vch.put(V.tensor_copy(qb3[:, :, 64:128], q3[:, :, 64:128]))
            else:
                sw_src = qsrc[:, 0:nc_cols].rearrange("p (u two) -> p u two", two=2)[:, :, ::-1]
                vch.put(V.tensor_copy(q_sw[:, 0:nh, :].rearrange("p h d -> p (h d)").rearrange("p (u two) -> p u two", two=2), sw_src))
                cosb = rq_buf[:, par, 0, :].unsqueeze(1).broadcast_to([128, nh, 128])
                sinb = rq_buf[:, par, 1, :].unsqueeze(1).broadcast_to([128, nh, 128])
                vch.put(V.tensor_mul(qr1[:, 0:nh, :], q3, cosb))
                vch.wait()
                vch.put(V.tensor_mul(q_sw[:, 0:nh, :], q_sw[:, 0:nh, :], sinb))
                vch.wait()
                vch.put(V.tensor_add(qb3[:, :, :], qr1[:, 0:nh, :], q_sw[:, 0:nh, :]))
            vch.pute(V.engine_nop(), g("s_qb"))
            rope_done.append(g("s_qb").n)
            g("s_qb").wait(TE)
            g("s_Tb").inc(transpose16(q_bf[:, 0:nc_cols], n=nh))
            g("s_Tb").wait(SC)
            sch.wait()
            sch.pute(SC.copy(qT_arr[:, t, h0:h0 + nh, :].rearrange("p h q -> p (h q)"),
                             Tps[:, 0:nc_cols]), g("s_TevB"))
            tb_done.append(g("s_TevB").n)
            if t + 2 < S:
                load(t + 2)

    def emit_p3_full(wlat, wkv, knwbc, kT_g0, v_g0, kvsc, xb, xnscr,
                     kf32, ksw, rop1, rop2, knb, rfb, junkL, junkK, kTo, vo, junkL2):
        nt = NT
        sxp = (g("s_x0"), g("s_x1"))
        s_xn, s_ss = g("s_xn"), g("s_ss")
        bTa, bTevA, bpj, bpe = g("s_Ta").n, g("s_TevA").n, g("s_pj").n, g("s_pe").n
        bknb, bTb, bTevB = g("s_knb").n, g("s_Tb").n, g("s_TevB").n
        bxn = s_xn.n
        ln_done, rope_done, kss_done, kvm_done, xload_done, kvd_done = [], [], [], [], [], []
        ks_done, vs_done = [], []
        sksp = (g("s_ks0"), g("s_ks1"))
        svsp = (g("s_vs0"), g("s_vs1"))

        def load(j):
            par = j % 2
            sx = sxp[par]
            if j >= 2:
                s_xn.wait(SY, ln_done[j - 2][0])
                s_ss.wait(SY, ln_done[j - 2][1])
                g("s_knb").wait(SY, bknb + j - 1)
            sx.inc(SY.dma_start(out=xb[:, par, :], in_=x_full[128 * j:128 * (j + 1), :]), 16)
            sx.inc(SY.dma_start(out=rfb[:, par, 0, :], in_=cosF_d[128 * j:128 * (j + 1), :]), 16)
            sx.inc(SY.dma_start(out=rfb[:, par, 1, :], in_=sinF_d[128 * j:128 * (j + 1), :]), 16)
            xload_done.append(sx.n)

        TpsB = PSB[:, :].bitcast(BF16)

        def tens_xn_mm(j_expr, par):
            TE.wait_ge(g("s_xn").h, bxn + j_expr + 1)
            TE.wait_ge(g("s_TevA").h, bTevA + j_expr)
            transpose16(xnscr[:, par, :]).then_inc(g("s_Ta").h, 1)
            TE.wait_ge(g("s_TevA").h, bTevA + j_expr + 1)
            TE.wait_ge(g("s_pe").h, bpe + 2 * j_expr)
            for c in range(16):
                TE.ldweights(xnT_scr[:, c, :])
                TE.matmul(PSA[:, par, 0:512], xnT_scr[:, c, :], wkv[:, c, 0:512],
                          start=(c == 0), stop=False, skip_group_check=True)
                TE.matmul(PSA[:, par, 512:1024], xnT_scr[:, c, :], wkv[:, c, 512:1024],
                          start=(c == 0), stop=False, skip_group_check=True)
                TE.matmul(PSB[:, 512 * par:512 * par + 128], xnT_scr[:, c, :], wlat[:, c, :],
                          start=(c == 0), stop=False, skip_group_check=True)
            TE.ldweights(ones_row[0:1, :])
            TE.matmul(PSB[:, 512 * par:512 * par + 128], ones_row[0:1, :], brow_lat[0:1, :],
                      start=False, stop=True, skip_group_check=True).then_inc(g("s_pj").h, 1)

        def tens_kt(j_expr, par):
            base = 1024 * (1 - par) + 256
            TE.wait_ge(g("s_knb").h, bknb + j_expr + 1)
            TE.wait_ge(g("s_TevB").h, bTevB + j_expr - 1)
            tl = None
            for a in range(4):
                tl = TE.transpose(TpsB[:, base + 128 * a:base + 128 * (a + 1)],
                                  knb[:, par, 128 * a:128 * (a + 1)], ident[:])
            tl.then_inc(g("s_Tb").h, 1)

        SY.sem_inc(g("s_pe").h, 2); g("s_pe").n += 2
        g("s_cg").wait(TE)
        g("s_wf").wait(TE)
        with reg_reclaim(), TE.Fori(0, nt // 2) as m:
            tens_xn_mm(m * 2, 0)
            tens_xn_mm(m * 2 + 1, 1)
            tens_kt(m * 2, 0)
            tens_kt(m * 2 + 1, 1)
        g("s_Ta").n += nt
        g("s_pj").n += nt
        g("s_Tb").n += nt

        load(0)
        load(1)
        st_of = lambda par, c: stat[:, par, c:c + 1]
        s1_done, s3_done = [], []

        def v1_ops(j):
            par = j % 2

            def a1():
                sxp[par].wait(V, xload_done[j])
                vch.put(V.reduce_sum(st_of(par, 0), xb[:, par, :], axis=AX.X))
            return [a1]

        def s1_ops(j):
            par = j % 2

            def a1():
                sxp[par].wait(SC, xload_done[j])
                sch.pute(SC.activation((junkL, junkL2)[par][:], xb[:, par, :], AF.Square,
                                       accum_out=st_of(par, 1)), g("s_ss"))
                s1_done.append(s_ss.n)
            return [a1]

        def v2_ops(j):
            par = j % 2
            st = lambda c: st_of(par, c)
            ops = []
            ops.append(lambda: (s_ss.wait(V, s1_done[j]),
                                vch.put(V.tensor_scalar(st(2), st(0), 1.0 / D, None, ALU.mult))))
            ops.append(lambda: vch.put(V.tensor_scalar(st(3), st(1), 1.0 / D, None, ALU.mult)))
            ops.append(lambda: vch.put(V.scalar_tensor_tensor(st(4), st(2), st(2), st(3),
                                                              ALU.mult, ALU.subtract)))
            ops.append(lambda: vch.put(V.tensor_scalar(st(4), st(4), -1.0, 1e-5, ALU.mult, ALU.add)))
            ops.append(lambda: vch.put(V.reciprocal(st(5), st(4))))
            ops.append(lambda: vch.put(V.tensor_scalar(st(8), st(5), -1e-5, 1.0 + 1e-6,
                                                       ALU.mult, ALU.add)))
            ops.append(lambda: vch.put(V.reciprocal(st(9), st(8))))
            ops.append(lambda: (V.engine_nop(), V.claim(g("s_hop"))))
            return ops

        def s2_ops(j):
            par = j % 2

            def a1():
                g("s_hop").wait(SC)
                sch.put(SC.activation(st_of(par, 6), st_of(par, 5), AF.Sqrt))

            def a2():
                sch.pute(SC.activation(st_of(par, 10), st_of(par, 9), AF.Sqrt), g("s_hop2"))
            return [a1, a2]

        def v3_ops(j):
            par = j % 2
            st = lambda c: st_of(par, c)

            def a1():
                g("s_hop2").wait(V)
                vch.put(V.tensor_scalar(st(7), st(2), st(6), -1.0, ALU.mult, ALU.mult))

            def a2():
                if j >= 2:
                    g("s_Ta").wait(V, bTa + j - 1)
                vch.pute(V.tensor_scalar(xnscr[:, par, :], xb[:, par, :], st(6), st(7),
                                         ALU.mult, ALU.add), g("s_xn"))
                ln_done.append((s_xn.n, s1_done[j]))
            return [a1, a2]

        def s3_ops(j):
            par = j % 2
            st = lambda c: st_of(par, c)
            ops = []

            def a1():
                g("s_Ta").wait(SC, bTa + j + 1)
                sch.pute(SC.copy(xnT_scr[:, :, :].rearrange("p c q -> p (c q)"), Tps[:, :]),
                         g("s_TevA"))
            ops.append(a1)

            def a2():
                SC.wait_ge(g("s_pj").h, bpj + j + 1)
                if j >= 2:
                    (g("s_kvd0"), g("s_kvd1"))[par].wait(SC, kvd_done[j - 2])
                else:
                    (g("s_kvd0"), g("s_kvd1"))[par].wait(SC)
                sch.pute(SC.copy(kvsc[:, par, :], PSB[:, 512 * par:512 * par + 128]), g("s_gen"))
                g("s_gen").wait(SY)
                skv = (g("s_kvd0"), g("s_kvd1"))[par]
                skv.inc(SY.dma_start(out=kv_dram[128 * j:128 * (j + 1), :], in_=kvsc[:, par, :]), 16)
                kvd_done.append(skv.n)
            ops.append(a2)

            def a3():
                if j >= 2:
                    g("s_kssv").wait(SC, kss_done[j - 2])
                sch.put(SC.activation(junkK[:, par, 0:128], PSB[:, 512 * par:512 * par + 128],
                                      AF.Square, accum_out=st(12)))
            ops.append(a3)
            ops.append(lambda: sch.put(SC.copy(kf32[:, par, :], PSA[:, par, 0:512])))

            def a5():
                sch.pute(SC.activation(junkK[:, par, :], PSA[:, par, 0:512], AF.Square,
                                       accum_out=st(11)), g("s_pe"))
                s3_done.append(g("s_pe").n)
            ops.append(a5)
            return ops

        def v4_ops(j):
            par = j % 2
            st = lambda c: st_of(par, c)
            ops = []

            def a1():
                g("s_pe").wait(V, s3_done[j])
                vch.put(V.tensor_scalar(v_g0[:, j, :, :].rearrange("p a d -> p (a d)"),
                                        PSA[:, par, 512:768], st(10), None, ALU.mult))
            ops.append(a1)

            def a2():
                if j >= 2:
                    svsp[par].wait(V, vs_done[j - 2])
                else:
                    svsp[par].wait(V)
                vch.pute(V.tensor_scalar(vo[:, par, :], PSA[:, par, 768:1024], st(10), None,
                                         ALU.mult), g("s_vo"))
                g("s_vo").wait(SY)
                svs = svsp[par]
                svs.inc(SY.dma_start(out=v1_dram[:, j, :, :].rearrange("p a d -> p (a d)"),
                                     in_=vo[:, par, :]), 16)
                vs_done.append(svs.n)
            ops.append(a2)
            ops.append(lambda: (V.engine_nop(), V.claim(g("s_pe"))))

            def a4():
                vch.put(V.tensor_max(kssh[:, 4 + 8 * par:5 + 8 * par],
                                     kssh[:, 4 + 8 * par:5 + 8 * par], st(12)))
                kvm_done.append(g("s_kvmax").n)
            ops.append(a4)
            ops.append(lambda: vch.put(V.tensor_scalar(st(13), st(10), st(10), None, ALU.mult)))
            ops.append(lambda: vch.put(V.tensor_scalar(st(14), st(11), st(13), None, ALU.mult)))
            ops.append(lambda: vch.put(V.tensor_scalar(st(14), st(14), 1.0 / KW, 1e-6,
                                                       ALU.mult, ALU.add)))
            ops.append(lambda: vch.put(V.reciprocal(st(15), st(14))))
            ops.append(lambda: (V.engine_nop(), V.claim(g("s_hop"))))
            return ops

        def s4_ops(j):
            par = j % 2

            def a1():
                g("s_hop").wait(SC)
                sch.pute(SC.activation(st_of(par, 14), st_of(par, 15), AF.Sqrt), g("s_hop2"))
            return [a1]

        def v5_ops(j):
            par = j % 2
            st = lambda c: st_of(par, c)
            ops = []

            def a1():
                g("s_hop2").wait(V)
                vch.put(V.tensor_scalar(st(15), st(14), st(10), None, ALU.mult))
            ops.append(a1)
            ops.append(lambda: vch.put(V.scalar_tensor_tensor(kf32[:, par, :], kf32[:, par, :],
                                                              st(15), knwbc[:, :],
                                                              ALU.mult, ALU.mult)))
            ops.append(lambda: (V.engine_nop(), V.claim(g("s_hop"))))
            return ops

        def s5_ops(j):
            par = j % 2

            def a1():
                g("s_hop").wait(SC)
                sch.pute(SC.activation(junkK[:, par, :], kf32[:, par, :], AF.Square), g("s_hop2"))
            return [a1]

        def v6_ops(j):
            par = j % 2
            k3 = kf32[:, par, :].rearrange("p (a d) -> p a d", a=4)
            kb3 = knb[:, par, :].rearrange("p (a d) -> p a d", a=4)
            rot = k3[:, :, 0:64]
            sw_src = rot.rearrange("p a (b c) -> p a b c", b=2)[:, :, ::-1, :]
            cosb = rfb[:, par, 0, :].unsqueeze(1).broadcast_to([128, 4, 64])
            sinb = rfb[:, par, 1, :].unsqueeze(1).broadcast_to([128, 4, 64])
            kt4 = ktmp4[:, 4 * par:4 * par + 4]
            ops = []

            def a1():
                g("s_hop2").wait(V)
                vch.put(V.tensor_reduce(kt4.rearrange("p (a o) -> p a o", o=1),
                                        junkK[:, par, :].rearrange("p (a d) -> p a d", a=4),
                                        AX.X, ALU.add))
            ops.append(a1)

            def a2():
                vch.put(V.tensor_max(kssh[:, 8 * par:8 * par + 4], kssh[:, 8 * par:8 * par + 4],
                                     kt4))
            ops.append(a2)

            def a3():
                vch.pute(V.engine_nop(), g("s_kssv"))
                kss_done.append(g("s_kssv").n)
            ops.append(a3)
            ops.append(lambda: vch.put(V.tensor_copy(
                ksw[:, par, :, :].rearrange("p a (b c) -> p a b c", b=2), sw_src)))
            ops.append(lambda: vch.put(V.tensor_mul(rop1[:, par, :, :], rot, cosb)))
            ops.append(lambda: vch.put(V.tensor_mul(rop2[:, par, :, :], ksw[:, par, :, :], sinb)))
            ops.append(lambda: vch.put(V.tensor_add(kb3[:, :, 0:64], rop1[:, par, :, :],
                                                    rop2[:, par, :, :])))
            ops.append(lambda: vch.put(V.tensor_copy(kb3[:, :, 64:128], k3[:, :, 64:128])))

            def a9():
                vch.pute(V.engine_nop(), g("s_knb"))
                rope_done.append(g("s_knb").n)
            ops.append(a9)
            return ops

        def s6_ops(j):
            par = j % 2
            base = 1024 * (1 - par) + 256
            TpsB2 = PSB[:, :].bitcast(BF16)
            ops = []

            def a1():
                SC.wait_ge(g("s_Tb").h, bTb + j + 1)
                if j >= 2:
                    sksp[par].wait(SC, ks_done[j - 2])
                else:
                    sksp[par].wait(SC)
                sch.pute(SC.copy(kTo[:, par, :], TpsB2[:, base + 256:base + 512]), g("s_ko"))
                g("s_ko").wait(SY)
                sks = sksp[par]
                sks.inc(SY.dma_start(out=kT1_dram[:, j, :, :].rearrange("p a q -> p (a q)"),
                                     in_=kTo[:, par, :]), 16)
                ks_done.append(sks.n)
            ops.append(a1)
            ops.append(lambda: sch.pute(SC.copy(kT_g0[:, :, j, :],
                                                TpsB2[:, base:base + 256].rearrange(
                                                    "p (a q) -> p a q", a=2)), g("s_TevB")))
            return ops

        def pair(mk, j0, j1):
            for f0, f1 in zip(mk(j0), mk(j1)):
                f0()
                f1()

        V.lb = 2
        SC.lb = 2
        for j0 in range(0, nt, 2):
            j1 = j0 + 1
            pair(v1_ops, j0, j1)
            pair(s1_ops, j0, j1)
            pair(v2_ops, j0, j1)
            pair(s2_ops, j0, j1)
            pair(v3_ops, j0, j1)
            pair(s3_ops, j0, j1)
            pair(v4_ops, j0, j1)
            pair(s4_ops, j0, j1)
            pair(v5_ops, j0, j1)
            pair(s5_ops, j0, j1)
            pair(v6_ops, j0, j1)
            pair(s6_ops, j0, j1)
            if j0 + 2 < nt:
                load(j0 + 2)
            if j1 + 2 < nt:
                load(j1 + 2)
        vch.wait()
        sch.wait()
        V.lb = 1
        SC.lb = 1


    def emit_attn_pass(br, t, g2, lo, hi, Et, qT_arr, spill_O):
        ffn = br == "ffn"
        sfx = Et - NSFX
        ncol = 2 * (0 if ffn else 1)
        assert lo == 0
        qk, exp, mk, pv = g("s_qk"), g("s_exp"), g("s_mk"), g("s_pv")
        stgp = (g("s_stg0"), g("s_stg1"), g("s_stg2"), g("s_stg3"))
        bqk, bexp, bmk, bpv = qk.n, exp.n, mk.n, pv.n
        bstg = tuple(s.n for s in stgp)
        qk_after = lambda j: bqk + (j - lo + 1)
        exp_after = lambda j: bexp + (j - lo + 1)
        pv_after = lambda j: bpv + (j - lo + 1)
        stg_after = lambda j: bstg[(j - lo) % 4] + 32 * ((j - lo) // 4 + 1)

        def ksrc(j):
            if ffn:
                return kT_g0v[:, :, j, :] if g2 == 0 else kT1_dram[:, j, :, :].rearrange("p a q -> p (a q)")
            return kvTv[:, j, :]

        def vsrc(j):
            if ffn:
                return v_g0v[:, j, :, :] if g2 == 0 else v1_dram[:, j, :, :].rearrange("p a d -> p (a d)")
            return kv_rawv[:, j, :]

        for j in range(lo, hi):
            slot = (j - lo) % 4
            stg = stgp[slot]
            if j - 4 >= lo:
                qk.wait(SY, qk_after(j - 4))
                pv.wait(SY, pv_after(j - 4))
            else:
                qk.wait(SY, bqk)
                pv.wait(SY, bpv)
            if ffn:
                stg.inc(SY.dma_start(out=kstg[:, slot, :, :].rearrange("p a q -> p (a q)"), in_=ksrc(j)), 16)
                stg.inc(SY.dma_start(out=vstg[:, slot, :, :].rearrange("p a q -> p (a q)"), in_=vsrc(j)), 16)
            else:
                stg.inc(SY.dma_start(out=kstg[:, slot, 0, :], in_=ksrc(j)), 16)
                stg.inc(SY.dma_start(out=vstg[:, slot, 0, :], in_=vsrc(j)), 16)

        def rhs_q(a):
            return qT_arr[:, t, 8 * g2 + 4 * a:8 * g2 + 4 * a + 4, :].rearrange("p h q -> p (h q)")

        def emit_qk(j, par, slot, stg_wait=None, exp_wait=None):
            if stg_wait is not None:
                TE.wait_ge(stgp[slot].h, stg_wait)
                TE.wait_ge(exp.h, exp_wait)
            else:
                TE.wait_ge(stgp[slot].h, stg_after(j))
                if j - 2 >= lo:
                    TE.wait_ge(exp.h, exp_after(j - 2))
            last = None
            if ffn:
                for a in range(2):
                    TE.ldweights(kstg[:, slot, a, :])
                    last = TE.matmul(PSA[:, par, 512 * a:512 * (a + 1)], kstg[:, slot, a, :], rhs_q(a),
                                     start=True, stop=True, skip_group_check=True)
            else:
                TE.ldweights(kstg[:, slot, 0, :])
                for a in range(2):
                    last = TE.matmul(PSA[:, par, 512 * a:512 * (a + 1)], kstg[:, slot, 0, :], rhs_q(a),
                                     start=True, stop=True, skip_group_check=True)
            last.then_inc(qk.h, 1)

        def emit_pv(j, par, slot, first, masked, exp_wait=None):
            if exp_wait is not None:
                TE.wait_ge(exp.h, exp_wait)
            elif masked:
                TE.wait_ge(mk.h, bmk + (j - max(sfx, lo) + 1))
            else:
                TE.wait_ge(exp.h, exp_after(j))
            last = None
            if ffn:
                for a in range(2):
                    TE.ldweights(vstg[:, slot, a, :])
                    TE.matmul(PSB[:, 512 * a:512 * (a + 1)], vstg[:, slot, a, :],
                              pbuf[:, par, 512 * a:512 * (a + 1)],
                              start=first, stop=False, skip_group_check=True)
            else:
                TE.ldweights(vstg[:, slot, 0, :])
                for a in range(2):
                    TE.matmul(PSB[:, 512 * a:512 * (a + 1)], vstg[:, slot, 0, :],
                              pbuf[:, par, 512 * a:512 * (a + 1)],
                              start=first, stop=False, skip_group_check=True)
            TE.ldweights(ones_col[:, 0:1])
            for a in range(2):
                last = TE.matmul(PSC[0:1, 512 * a:512 * (a + 1)], ones_col[:, 0:1],
                                 pbuf[:, par, 512 * a:512 * (a + 1)],
                                 start=first, stop=False, skip_group_check=True)
            last.then_inc(pv.h, 1)

        g("s_sp").wait(TE)
        g("s_dnc").wait(TE)
        nfori = 4 * max(0, (min(sfx, hi) - (lo + 2)) // 4)
        fori_end = lo + 2 + nfori
        emit_qk(lo, lo % 2, 0)
        if lo + 1 < hi:
            emit_qk(lo + 1, (lo + 1) % 2, 1)
        emit_pv(lo, lo % 2, 0, True, lo >= sfx)
        if nfori > 0:
            with reg_reclaim(), TE.Fori(0, nfori // 4) as m:
                m4 = m * 4
                m32 = m * 32
                for k in range(4):
                    emit_qk(None, k % 2, (2 + k) % 4,
                            stg_wait=bstg[(2 + k) % 4] + (32 if k < 2 else 64) + m32,
                            exp_wait=bexp + 1 + k + m4)
                    emit_pv(None, (1 + k) % 2, (1 + k) % 4, False, False,
                            exp_wait=bexp + 2 + k + m4)
            qk.n += nfori
            pv.n += nfori
        for j in range(fori_end, hi):
            emit_qk(j, j % 2, (j - lo) % 4)
            emit_pv(j - 1, (j - 1) % 2, (j - 1 - lo) % 4, False, (j - 1) >= sfx)
        emit_pv(hi - 1, (hi - 1) % 2, (hi - 1 - lo) % 4, False, (hi - 1) >= sfx)
        qk.n = qk_after(hi - 1)
        pv.n = pv_after(hi - 1)

        for j in range(lo, hi):
            par = j % 2
            SC.wait_ge(qk.h, qk_after(j))
            if j - 2 >= lo:
                SC.wait_ge(pv.h, pv_after(j - 2))
            else:
                SC.wait_ge(pv.h, bpv)
            SC.activation(pbuf[:, par, :], PSA[:, par, :], AF.Exp,
                          bias=biasB[:, ncol + g2:ncol + g2 + 1], scale=ISQ)
            SC.claim(exp)
            if j >= sfx:
                V.wait_ge(exp.h, exp.n)
                V.tensor_mul(pbuf[:, par, :], pbuf[:, par, :], masksv[:, j - sfx, :])
                V.claim(mk)
        V.wait_ge(pv.h, pv.n)
        if spill_O is not None:
            vch.put(V.tensor_add(spill_O, spill_O, PSB[:, :]))
            vch.put(V.tensor_copy(lrow0[0:1, :], PSC[0:1, 0:1024]))
            vch.pute(V.engine_nop(), g("s_gen"))
            g("s_gen").wait(TE)
            il = None
            for h4 in range(8):
                il = TE.transpose(PSA_flat[:, h4:h4 + 1], lrow0[0:1, 128 * h4:128 * (h4 + 1)],
                                  ident32[0:1, 0:1])
            g("s_gen").inc(il)
            g("s_gen").wait(V)
            vch.put(V.tensor_add(lacc_tok[:, t, 8 * g2:8 * g2 + 8],
                                 lacc_tok[:, t, 8 * g2:8 * g2 + 8], PSA_flat[:, 0:8]))
            vch.pute(V.engine_nop(), g("s_sp"))

    def emit_dance(t, g2, o_src_psum, Osc_v):
        PSAbf = PSA_flat.bitcast(BF16)
        if o_src_psum:
            V.wait_ge(g("s_pv").h, g("s_pv").n)
            vch.put(V.tensor_copy(obuf[:, 0:1024], PSB[:, :]))
            vch.put(V.tensor_copy(lrow0[0:1, :], PSC[0:1, 0:1024]))
            vch.wait()
        else:
            vch.put(V.tensor_copy(obuf[:, 0:1024],
                                  Oaccv[:, t, 8 * g2:8 * g2 + 8, :].rearrange("p h q -> p (h q)")))
            vch.wait()
        vch.pute(V.engine_nop(), g("s_dnc"))
        g("s_dnc").wait(TE)
        i1 = None
        for h4 in range(8):
            i1 = TE.transpose(PSAbf[:, 2048 + 128 * h4:2048 + 128 * (h4 + 1)],
                              obuf[:, 128 * h4:128 * (h4 + 1)], ident[:])
        if o_src_psum:
            for h4 in range(8):
                i1 = TE.transpose(PSA_flat[:, h4:h4 + 1], lrow0[0:1, 128 * h4:128 * (h4 + 1)],
                                  ident32[0:1, 0:1])
        g("s_dnc").inc(i1)
        g("s_dnc").wait(V)
        if o_src_psum:
            vch.put(V.tensor_copy(lt_scr[:, 0:8], PSA_flat[:, 0:8]))
        else:
            vch.put(V.tensor_copy(lt_scr[:, 0:8], lacc_tok[:, t, 8 * g2:8 * g2 + 8]))
        vch.wait()
        vch.put(V.reciprocal(rl_scr[:, 0:8], lt_scr[:, 0:8]))
        vch.wait()
        for h4 in range(8):
            ii = V.tensor_scalar(
                Osc_v[:, t, 8 * g2 + h4, :], PSAbf[:, 2048 + 128 * h4:2048 + 128 * (h4 + 1)],
                rl_scr[:, h4:h4 + 1], None, ALU.mult)
            if h4 == 7:
                vch.pute(ii, g("s_dnc"))

    def emit_poproj(kind, Osc_v, wbig, x1_v, ACC_v, xb, junkL):
        if kind == "o":
            hh, nco = 0, 2048
        else:
            hh, nco = int(kind[-1]), 1024
        ncb = nco // 512
        xld = []
        ln_done = []

        def load(t):
            par = t % 2
            if t >= 2:
                g("s_xn").wait(SY, ln_done[t - 2][0])
                g("s_ss").wait(SY, ln_done[t - 2][1])
            sx = (g("s_x0"), g("s_x1"))[par]
            sx.inc(SY.dma_start(out=xb[:, par, :], in_=x_own[128 * t:128 * (t + 1), :]), 16)
            xld.append(sx.n)

        if kind == "o":
            load(0)
            if S > 1:
                load(1)
        pj_done = []
        for t in range(S):
            par = t % 2
            vch.wait()
            if t >= 1:
                V.wait_ge(g("s_pj").h, pj_done[t - 1])
            vch.pute(V.tensor_copy(ot_stage[:, :], Osc_v[:, t, :, :].rearrange("p h d -> p (h d)")),
                     g("s_x2"))
            g("s_x2").wait(TE)
            g("s_TevA").wait(TE)
            g("s_Ta").inc(transpose16(ot_stage[:, :]))
            g("s_Ta").wait(SC)
            sch.wait()
            sch.pute(SC.copy(otT_scr[:, :, :].rearrange("p c q -> p (c q)"), Tps[:, :]), g("s_TevA"))
            g("s_TevA").wait(TE)
            g("s_pe").wait(TE)
            last = None
            for c in range(16):
                TE.ldweights(otT_scr[:, c, :])
                for b in range(ncb):
                    last = TE.matmul(PSA_flat[:, 512 * b:512 * (b + 1)], otT_scr[:, c, :],
                                     wbig[:, c, 512 * b:512 * (b + 1)],
                                     start=(c == 0), stop=False, skip_group_check=True)
            if kind != "o":
                TE.ldweights(ones_row[0:1, :])
                for b in range(ncb):
                    last = TE.matmul(PSA_flat[:, 512 * b:512 * (b + 1)], ones_row[0:1, :],
                                     brow_proj[0:1, 1024 * hh + 512 * b:1024 * hh + 512 * (b + 1)],
                                     start=False, stop=True, skip_group_check=True)
            g("s_pj").inc(last)
            pj_done.append(g("s_pj").n)
            if kind == "o":
                sx = (g("s_x0"), g("s_x1"))[par]
                sx.wait(V, xld[t]); sx.wait(SC, xld[t])
                emit_ln(par, xb[:, par, :], want_rf1=False, xn_dst=None, junkL=junkL)
                st = lambda c: stat[:, par, c:c + 1]
                vch.put(V.tensor_scalar(q_fv[:, :], xb[:, par, :], st(6), st(7), ALU.mult, ALU.add))
                vch.pute(V.engine_nop(), g("s_xn"))
                ln_done.append((g("s_xn").n, g("s_ss").n))
                g("s_pj").wait(V)
                vch.pute(V.tensor_add(x1_v[:, t, :], q_fv[:, :], PSA_flat[:, :]), g("s_pe"))
                if t + 2 < S:
                    load(t + 2)
            else:
                g("s_pj").wait(V)
                vch.wait()
                vch.pute(V.tensor_add(ACC_v[:, t, 1024 * hh:1024 * (hh + 1)],
                                      PSA_flat[:, 0:1024],
                                      x1_v[:, t, 1024 * hh:1024 * (hh + 1)]), g("s_pe"))

    def emit_pr(kv_rawv, kvTv, rib, rsw, rr1, rr2, rkb):
        rld = []
        rope_done = []

        def load(j):
            par = j % 2
            if j >= 2:
                g("s_knb").wait(SY, rope_done[j - 2])
            sx = (g("s_x0"), g("s_x1"))[par]
            sx.inc(SY.dma_start(out=rib[:, par, 0, :], in_=cosI_d[128 * j:128 * (j + 1), :]), 16)
            sx.inc(SY.dma_start(out=rib[:, par, 1, :], in_=sinI_d[128 * j:128 * (j + 1), :]), 16)
            rld.append(sx.n)

        load(0)
        if NT > 1:
            load(1)
        bTb, bTevB = g("s_Tb").n, g("s_TevB").n
        for j in range(NT):
            par = j % 2
            (g("s_x0"), g("s_x1"))[par].wait(V, rld[j])
            src = kv_rawv[:, j, :]
            sw_src = src.rearrange("p (u two) -> p u two", two=2)[:, :, ::-1]
            vch.wait()
            vch.put(V.tensor_copy(rsw[:, :].rearrange("p (u two) -> p u two", two=2), sw_src))
            vch.put(V.tensor_mul(rr1[:, :], src, rib[:, par, 0, :]))
            vch.wait()
            vch.put(V.tensor_mul(rr2[:, :], rsw[:, :], rib[:, par, 1, :]))
            vch.wait()
            if j >= 1:
                V.wait_ge(g("s_Tb").h, bTb + j)
            vch.pute(V.tensor_add(rkb[:, :], rr1[:, :], rr2[:, :]), g("s_knb"))
            rope_done.append(g("s_knb").n)
            if j + 2 < NT:
                load(j + 2)
            g("s_knb").wait(TE)
            TE.wait_ge(g("s_TevB").h, bTevB + j)
            TE.transpose(Tps[:, 0:128], rkb[:, :], ident[:]).then_inc(g("s_Tb").h, 1)
            g("s_Tb").n += 1
            SC.wait_ge(g("s_Tb").h, g("s_Tb").n)
            sch.wait()
            sch.pute(SC.copy(kvTv[:, j, :], Tps[:, 0:128]), g("s_TevB"))

    def emit_p5pre(x1_v, tT_v, ln2bc, junkL, tstage):
        bTa0 = g("s_Ta").n
        for t in range(S):
            par = t % 2
            st = lambda c: stat[:, par, c:c + 1]
            if t >= 1:
                g("s_kssv").wait(SC)
            g("s_sp").wait(SC)
            sch.wait()
            sch.pute(SC.activation(junkL[:], x1_v[:, t, :], AF.Square, accum_out=st(11)), g("s_ss"))
            g("s_ss").wait(V)
            vch.wait()
            vch.put(V.tensor_scalar(st(3), st(11), 1.0 / D, 1e-6, ALU.mult, ALU.add))
            vch.wait()
            vch.put(V.reciprocal(st(4), st(3)))
            vch.wait()
            V.engine_nop()
            V.claim(g("s_hop")); g("s_hop").wait(SC)
            sch.pute(SC.activation(st(5), st(4), AF.Sqrt), g("s_hop2"))
            g("s_hop2").wait(V)
            vch.put(V.tensor_copy(r2s[:, t:t + 1], st(5)))
            vch.wait()
            if t >= 1:
                g("s_Ta").wait(V, bTa0 + t)
            vch.pute(V.scalar_tensor_tensor(tstage[:, :], x1_v[:, t, :], st(5), ln2bc[:, :],
                                            ALU.mult, ALU.mult), g("s_kssv"))
            g("s_kssv").wait(TE)
            g("s_TevA").wait(TE)
            g("s_Ta").inc(transpose16(tstage[:, :]))
            g("s_Ta").wait(SC)
            sch.wait()
            sch.pute(SC.copy(tT_v[:, t, :, :].rearrange("p c q -> p (c q)"), Tps[:, :]), g("s_TevA"))
        for t in range(S):
            g("s_TevA").wait(TE)
            g("s_pe").wait(TE)
            last = None
            for c in range(16):
                TE.ldweights(tT_v[:, t, c, :])
                last = TE.matmul(PSB[:, 0:8], tT_v[:, t, c, :], rw_sb[:, c, :],
                                 start=(c == 0), stop=(c == 15), skip_group_check=True)
            g("s_pj").inc(last)
            g("s_pj").wait(SC)
            if t >= 1:
                g("s_sg").wait(SC)
            sch.wait()
            sch.pute(SC.activation(qtmp16[:, 0:8], PSB[:, 0:8], AF.Sigmoid), g("s_hop"))
            g("s_hop").wait(V)
            vch.wait()
            vch.put(V.tensor_add(qtmp16[:, 8:16], qtmp16[:, 0:8], rbias_bc[:, :]))
            vch.pute(V.engine_nop(), g("s_pe"))
            vch.wait()
            vch.put(V.reduce_max(lt_scr[:, 0:1], qtmp16[:, 8:16], axis=AX.X))
            vch.wait()
            vch.put(V.tensor_scalar(rl_scr[:, 0:8], qtmp16[:, 8:16], lt_scr[:, 0:1], None, ALU.is_ge))
            vch.wait()
            vch.put(V.scalar_tensor_tensor(rl_scr[:, 8:16], rl_scr[:, 0:8], -1e30, qtmp16[:, 8:16],
                                           ALU.mult, ALU.add))
            vch.wait()
            vch.put(V.reduce_max(lt_scr[:, 1:2], rl_scr[:, 8:16], axis=AX.X))
            vch.wait()
            vch.put(V.tensor_scalar(lt_scr[:, 8:16], rl_scr[:, 8:16], lt_scr[:, 1:2], None, ALU.is_ge))
            vch.wait()
            vch.put(V.tensor_add(rl_scr[:, 0:8], rl_scr[:, 0:8], lt_scr[:, 8:16]))
            vch.wait()
            vch.pute(V.tensor_mul(rl_scr[:, 0:8], rl_scr[:, 0:8], qtmp16[:, 0:8]), g("s_sg"))
            vch.wait()
            vch.put(V.reduce_sum(lt_scr[:, 2:3], rl_scr[:, 0:8], axis=AX.X))
            vch.wait()
            vch.put(V.reciprocal(lt_scr[:, 3:4], lt_scr[:, 2:3]))
            vch.wait()
            vch.put(V.tensor_scalar(comb[:, t, :], rl_scr[:, 0:8], lt_scr[:, 3:4], None, ALU.mult))
            vch.wait()

    def emit_p5(tT_v, ACC_v, ws, act_tr, actT_scr, tTs_p, junk5):
        wgh, wuh, wdh = ws
        s_gu, s_dn = g("s_qk"), g("s_pv")
        swD = g("s_wgD")
        swA_p = (g("s_wgA"), g("s_wA1"))
        swB_p = (g("s_wgB"), g("s_wB1"))
        bgu0, bdn0 = s_gu.n, s_dn.n

        def load_gu(p):
            if p >= 16:
                return
            par = p % 2
            e, ih = divmod(p, 2)
            if p >= 2:
                GP.wait_ge(s_gu.h, bgu0 + S * (p - 1))
            swA_p[par].inc(GP.dma_start(out=wgh[:, par, :, :],
                                 in_=wg_d[e, :, 512 * ih:512 * (ih + 1)].rearrange("(c p) f -> p c f", p=128)), 16)
            swB_p[par].inc(GP.dma_start(out=wuh[:, par, :, :],
                                 in_=wu_d[e, :, 512 * ih:512 * (ih + 1)].rearrange("(c p) f -> p c f", p=128)), 16)

        load_gu(0)
        load_gu(1)
        for e in range(8):
            for ih in range(2):
                p = 2 * e + ih
                par = p % 2
                if p >= 1:
                    GP.wait_ge(s_dn.h, bdn0 + S * p)
                swD.inc(GP.dma_start(out=wdh[:, :, :],
                                     in_=wd_d[e, 512 * ih:512 * (ih + 1), :].rearrange("(c p) f -> p c f", p=128)), 16)
                load_gu(p + 2)
                bgu, bdn = s_gu.n, s_dn.n
                bmk, bx2 = g("s_mk").n, g("s_x2").n
                bTa, bTevA, bpe = g("s_Ta").n, g("s_TevA").n, g("s_pe").n
                vch.wait()
                vch.pute(V.tensor_copy(tTs_p[0][:, :, :].rearrange("p c q -> p (c q)"),
                                       tT_v[:, 0, :, :].rearrange("p c q -> p (c q)")),
                         g("s_x2"))
                TE.wait_ge(swA_p[par].h, 16 * (p // 2 + 1))
                TE.wait_ge(swB_p[par].h, 16 * (p // 2 + 1))
                swDn = swD.n

                def p5_body(m2, ks):
                    for k in ks:
                        TE.wait_ge(g("s_x2").h, bx2 + 1 + k + m2)
                        TE.wait_ge(g("s_mk").h, bmk - 1 + k + m2)
                        last = None
                        for c in range(16):
                            TE.ldweights(tTs_p[k][:, c, :])
                            TE.matmul(PSA[:, 0, 512 * k:512 * (k + 1)], tTs_p[k][:, c, :],
                                      wgh[:, par, c, :],
                                      start=(c == 0), stop=False, skip_group_check=True)
                            last = TE.matmul(PSA[:, 1, 512 * k:512 * (k + 1)], tTs_p[k][:, c, :],
                                             wuh[:, par, c, :],
                                             start=(c == 0), stop=False, skip_group_check=True)
                        last.then_inc(s_gu.h, 1)
                    for k in ks:
                        TE.wait_ge(g("s_mk").h, bmk + 1 + k + m2)
                        TE.wait_ge(g("s_pe").h, bpe + 2 * (k + m2))
                        lt2 = None
                        for c4 in range(4):
                            lt2 = TE.transpose(Tps[:, 128 * c4:128 * (c4 + 1)],
                                               act_tr[:, 128 * c4:128 * (c4 + 1)], ident[:])
                        lt2.then_inc(g("s_Ta").h, 1)
                        TE.wait_ge(g("s_TevA").h, bTevA + 1 + k + m2)
                        TE.wait_ge(swD.h, swDn)
                        last = None
                        for c4 in range(4):
                            TE.ldweights(actT_scr[:, c4, :])
                            for bb in range(4):
                                dst = PSB[:, 512 * bb:512 * (bb + 1)] if bb < 2 else \
                                    PSC[:, 512 * (bb - 2):512 * (bb - 1)]
                                last = TE.matmul(dst, actT_scr[:, c4, :],
                                                 wdh[:, c4, 512 * bb:512 * (bb + 1)],
                                                 start=(c4 == 0), stop=False, skip_group_check=True)
                        last.then_inc(s_dn.h, 1)

                if S >= 2:
                    with reg_reclaim(), TE.Fori(0, S // 2) as m:
                        p5_body(m * 2, (0, 1))
                else:
                    p5_body(0, (0,))
                s_gu.n = bgu + S
                s_dn.n = bdn + S
                g("s_Ta").n = bTa + S
                def spill(t):
                    V.wait_ge(s_dn.h, bdn + t + 1)
                    vch.wait()
                    vch.put(V.tensor_add(ACC_v[:, t, 0:1024], ACC_v[:, t, 0:1024], PSB[:, :]))
                    vch.pute(V.tensor_add(ACC_v[:, t, 1024:2048], ACC_v[:, t, 1024:2048], PSC[:, :]),
                             g("s_pe"), 2)

                for t in range(S):
                    tp = t % 2
                    SC.wait_ge(s_gu.h, bgu + t + 1)
                    if t >= 1:
                        SC.wait_ge(g("s_mk").h, bmk + t)
                    sch.wait()
                    sch.pute(SC.activation(junk5[:, 0:512], PSA[:, 0, 512 * tp:512 * (tp + 1)],
                                           AF.Silu), g("s_hop"))
                    g("s_hop").wait(V)
                    vch.wait()
                    if t >= 1:
                        V.wait_ge(g("s_Ta").h, bTa + t)
                    vch.pute(V.scalar_tensor_tensor(act_tr[:, :], PSA[:, 1, 512 * tp:512 * (tp + 1)],
                                                    comb[:, t, e:e + 1], junk5[:, 0:512],
                                                    ALU.mult, ALU.mult), g("s_mk"))
                    if t + 1 < S:
                        vch.wait()
                        vch.pute(V.tensor_copy(tTs_p[(t + 1) % 2][:, :, :].rearrange("p c q -> p (c q)"),
                                               tT_v[:, t + 1, :, :].rearrange("p c q -> p (c q)")),
                                 g("s_x2"))
                    if t >= 1:
                        spill(t - 1)
                    SC.wait_ge(g("s_Ta").h, bTa + t + 1)
                    if t >= 1:
                        SC.wait_ge(s_dn.h, bdn + t)
                    sch.wait()
                    sch.pute(SC.copy(actT_scr[:, 0:4, :].rearrange("p c q -> p (c q)"),
                                                    Tps[:, 0:512]), g("s_TevA"))
                spill(S - 1)

    def emit_p5f(ACC_v, xb, outs):
        xld = []
        out_done = []
        sop = (g("s_out0"), g("s_out1"))

        def load(t):
            par = t % 2
            sx = (g("s_x0"), g("s_x1"))[par]
            sx.inc(SY.dma_start(out=xb[:, par, :], in_=x_own[128 * t:128 * (t + 1), :]), 16)
            xld.append(sx.n)

        load(0)
        if S > 1:
            load(1)
        for t in range(S):
            par = t % 2
            (g("s_x0"), g("s_x1"))[par].wait(V, xld[t])
            if t >= 2:
                sop[par].wait(V, out_done[t - 2])
            vch.wait()
            vch.pute(V.tensor_add(outs[:, par, :], ACC_v[:, t, :], xb[:, par, :]), g("s_r2"))
            g("s_r2").wait(SY)
            sop[par].inc(SY.dma_start(out=out_d[128 * t:128 * (t + 1), :], in_=outs[:, par, :]), 16)
            out_done.append(sop[par].n)
            if t + 2 < S:
                load(t + 2)
        SY.wait_ge(sop[0].h, sop[0].n)
        SY.wait_ge(sop[1].h, sop[1].n)

    kv_dram = nc.dram_tensor("kv_scratch", [T, 128], BF16).ap()
    kT1_dram = nc.dram_tensor("kT1_scratch", [128, NT, 2, 128], BF16).ap()
    v1_dram = nc.dram_tensor("v1_scratch", [128, NT, 2, 128], BF16).ap()
    rw_sb = sbuf("rw_sb", [128, 16, 8], BF16)

    xb = AR.alloc([128, 2, D], F32)
    xnscr = AR.alloc([128, 2, D], BF16)
    junkL = AR.alloc([128, D], BF16)
    qfT = AR.alloc([128, S, 16, 128], BF16)
    wqp = AR.alloc([128, 16, 2048], BF16, hi=True)
    wstg = AR.alloc([128, 2, 2048], F32, hi=True)
    qnwbc_a = AR.alloc([128, D], F32, hi=True)
    q_f = AR.alloc([128, D], F32, hi=True)
    q_sw = AR.alloc([128, 16, 128], F32, hi=True)
    qr1 = AR.alloc([128, 16, 128], F32, hi=True)
    q_bf = AR.alloc([128, D], BF16, hi=True)
    rqb = AR.alloc([128, 2, 2, 128], F32, hi=True)
    s_c0.inc(SY.dma_start(out=qnwbc_a[:, :], in_=qnorm_d.unsqueeze(0).partition_broadcast(128)), 16)
    fold_done = []
    wload = []
    swp = (g("s_w0"), g("s_w1"))
    s_wf = g("s_wf")

    def wq_load(c):
        if c >= 2:
            s_wf.wait(SY, fold_done[c - 2])
        swp[c % 2].inc(SY.dma_start(out=wstg[:, c % 2, :], in_=wq_d[128 * c:128 * (c + 1), :]), 16)
        wload.append(swp[c % 2].n)

    wq_load(0); wq_load(1)
    for c in range(16):
        swp[c % 2].wait(V, wload[c])
        vch.wait()
        vch.pute(V.tensor_scalar(wqp[:, c, :], wstg[:, c % 2, :], ln1_sb[:, c:c + 1], None, ALU.mult),
                 s_wf)
        fold_done.append(s_wf.n)
        if c + 2 < 16:
            wq_load(c + 2)
    s_c0.wait(V)
    g("s_cg").wait(V)
    g("s_cg").wait(V)
    if PROBE: PROBE("pre-P1", nc)
    emit_qbuild("ffn", wqp, qfT, xb, xnscr, q_f, q_sw, qr1, q_bf, rqb, junkL, qnwbc=qnwbc_a)
    if PROBE: PROBE("post-P1", nc)
    nc.all_engine_barrier()
    dump("qfT", qfT[:, :, :, :].rearrange("p s h q -> p (s h q)"), [128, S * 16 * 128], BF16)
    for b_ in (wqp, wstg, qnwbc_a, q_f, q_sw, qr1, q_bf, rqb):
        AR.free(b_)

    def p3_weights():
        wlat = AR.alloc([128, 16, 128], BF16, hi=True)
        wkv = AR.alloc([128, 16, 1024], BF16, hi=True)
        knwbc = AR.alloc([128, KW], F32, hi=True)
        wstg2 = AR.alloc([128, 2, 1024], F32, hi=True)
        g("s_cg").inc(GP.dma_start(out=wlat[:, :, :], in_=W_latent_d.rearrange("(c p) f -> p c f", p=128)), 16)
        s_c0.inc(SY.dma_start(out=knwbc[:, :], in_=knorm_d.unsqueeze(0).partition_broadcast(128)), 16)
        fold2, wload2 = [], []
        swp = (g("s_w0"), g("s_w1"))

        def kv_load(c):
            if c >= 2:
                s_wf.wait(SY, fold2[c - 2])
            swp[c % 2].inc(SY.dma_start(out=wstg2[:, c % 2, 0:512], in_=wk_d[128 * c:128 * (c + 1), :]), 16)
            swp[c % 2].inc(SY.dma_start(out=wstg2[:, c % 2, 512:1024], in_=wv_d[128 * c:128 * (c + 1), :]), 16)
            wload2.append(swp[c % 2].n)

        kv_load(0); kv_load(1)
        for c in range(16):
            swp[c % 2].wait(V, wload2[c])
            vch.wait()
            vch.pute(V.tensor_scalar(wkv[:, c, :], wstg2[:, c % 2, :], ln1_sb[:, c:c + 1], None, ALU.mult),
                     s_wf)
            fold2.append(s_wf.n)
            if c + 2 < 16:
                kv_load(c + 2)
        return wlat, wkv, knwbc, wstg2

    kTo_a = AR.alloc([128, 2, 256], BF16, hi=True)
    vo_a = AR.alloc([128, 2, 256], BF16, hi=True)
    kf32 = AR.alloc([128, 2, KW], F32, hi=True)
    ksw = AR.alloc([128, 2, 4, 64], F32, hi=True)
    rop1 = AR.alloc([128, 2, 4, 64], F32, hi=True)
    rop2 = AR.alloc([128, 2, 4, 64], F32, hi=True)
    knb = AR.alloc([128, 2, KW], BF16, hi=True)
    rfb = AR.alloc([128, 2, 2, 64], F32, hi=True)
    junkK = AR.alloc([128, 2, KW], BF16, hi=True)
    junkL2 = AR.alloc([128, D], BF16, hi=True)
    wlat, wkv, knwbc, wstg2 = p3_weights()
    s_c0.wait(V)
    g("s_cg").wait(V)
    kT_g0 = AR.alloc([128, 2, NT, 128], BF16)
    v_g0 = AR.alloc([128, NT, 2, 128], BF16)
    kT_g0v, v_g0v = kT_g0, v_g0
    emit_p3_full(wlat, wkv, knwbc, kT_g0, v_g0, kvsc, xb, xnscr,
                 kf32, ksw, rop1, rop2, knb, rfb, junkL, junkK, kTo_a, vo_a, junkL2)
    if PROBE: PROBE("post-P3", nc)
    nc.all_engine_barrier()
    dump("kT0", kT_g0[:, :, :, :].rearrange("p a t q -> p (a t q)"), [128, 2 * NT * 128], BF16)
    dump("v0", v_g0[:, :, :, :].rearrange("p t a d -> p (t a d)"), [128, 2 * NT * 128], BF16)
    emit_B_assembly("ffn")
    nc.all_engine_barrier()
    for b_ in (wlat, wkv, knwbc, wstg2, kf32, ksw, rop1, rop2, knb, rfb, junkK,
               junkL2, kTo_a, vo_a):
        AR.free(b_)

    masksA = AR.alloc([128, NSFX, 1024], BF16)
    pbuf = AR.alloc([128, 2, 1024], BF16, hi=True)
    kstg = AR.alloc([128, 4, 2, 128], BF16, hi=True)
    vstg = AR.alloc([128, 4, 2, 128], BF16, hi=True)
    lrow0 = AR.alloc([1, 1024], F32, hi=True)
    OscF = AR.alloc([128, S, 16, 128], BF16)
    obuf_a = AR.alloc([128, 2048], BF16, hi=True)
    obuf = obuf_a
    masksv = masksA
    g("s_ks0").wait(SY); g("s_ks1").wait(SY)
    g("s_vs0").wait(SY); g("s_vs1").wait(SY)
    s_c0.inc(SY.dma_start(out=masksA[:, :, :], in_=masks_d), 16)
    s_c0.wait(V); s_c0.wait(TE); s_c0.wait(SC)
    for g2 in range(2):
        for t in range(S):
            Et = C * (t + 1)
            if PROBE: PROBE(f"pre-pass-{g2}-{t}", nc)
            emit_attn_pass("ffn", t, g2, 0, Et, Et, qfT, None)
            emit_dance(t, g2, True, OscF)
        if g2 == 0:
            nc.all_engine_barrier()
            AR.free(kT_g0)
            AR.free(v_g0)
    nc.all_engine_barrier()
    AR.free(qfT)
    for b_ in (pbuf, kstg, vstg, lrow0, obuf_a):
        AR.free(b_)
    dump("OscF", OscF[:, :, :, :].rearrange("p s h q -> p (s h q)"), [128, S * 16 * 128], BF16)

    x1_a = AR.alloc([128, S, D], BF16)
    q_fv = AR.alloc([128, D], F32, hi=True)
    wo_sb = AR.alloc([128, 16, 2048], BF16, hi=True)
    g("s_cg").inc(GP.dma_start(out=wo_sb[:, :, :], in_=wo_d.rearrange("(c p) f -> p c f", p=128)), 16)
    s_c0.wait(TE)
    g("s_cg").wait(TE)
    emit_poproj("o", OscF, wo_sb, x1_a, None, xb, junkL)
    nc.all_engine_barrier()
    AR.free(wo_sb)
    AR.free(OscF)
    AR.free(q_fv)
    dump("x1", x1_a[:, :, :].rearrange("p s d -> p (s d)"), [128, S * D], BF16)

    WdH = AR.alloc([128, 16, 1024], BF16, hi=True)
    qaT = AR.alloc([128, S, 16, 128], BF16)
    brow_d = AR.alloc([1, D], BF16, hi=True)
    g("s_cg").inc(GP.dma_start(out=brow_d[0:1, :], in_=b_d_d.unsqueeze(0)), 16)
    q_f2 = AR.alloc([128, 1024], F32, hi=True)
    q_sw2 = AR.alloc([128, 8, 128], F32, hi=True)
    qr12 = AR.alloc([128, 8, 128], F32, hi=True)
    q_bf2 = AR.alloc([128, 1024], BF16, hi=True)
    rqb2 = AR.alloc([128, 2, 2, 128], F32, hi=True)
    for hh in range(2):
        g("s_cg").inc(GP.dma_start(out=WdH[:, :, :],
                              in_=W_d_d[:, 1024 * hh:1024 * (hh + 1)].rearrange("(c p) f -> p c f", p=128)), 16)
        s_c0.wait(TE)
        g("s_cg").wait(TE)
        emit_qbuild("attn", WdH, qaT, xb, xnscr, q_f2, q_sw2, qr12, q_bf2, rqb2, junkL,
                    h0=8 * hh, nh=8)
        nc.all_engine_barrier()
    emit_B_assembly("attn")
    nc.all_engine_barrier()
    for b_ in (WdH, q_f2, q_sw2, qr12, q_bf2, rqb2, brow_d, xnscr):
        AR.free(b_)
    dump("qaT", qaT[:, :, :, :].rearrange("p s h q -> p (s h q)"), [128, S * 16 * 128], BF16)
    OscA = AR.alloc([128, S, 16, 128], BF16)

    kv_rawA = AR.alloc([128, NT, 128], BF16)
    kvTA = AR.alloc([128, NT, 128], BF16)
    kv_rawv, kvTv = kv_rawA, kvTA
    rib = AR.alloc([128, 2, 2, 128], F32, hi=True)
    rsw = AR.alloc([128, 128], F32, hi=True)
    rr1 = AR.alloc([128, 128], F32, hi=True)
    rr2 = AR.alloc([128, 128], F32, hi=True)
    rkb = AR.alloc([128, 128], BF16, hi=True)
    g("s_kvd0").wait(SY)
    g("s_kvd1").wait(SY)
    s_c0.inc(SY.dma_start(out=kv_rawA[:, :, :], in_=kv_dram.rearrange("(t p) d -> p t d", p=128)), 16)
    s_c0.wait(V)
    g("s_cg").wait(V); s_c0.wait(TE)
    emit_pr(kv_rawA, kvTA, rib, rsw, rr1, rr2, rkb)
    nc.all_engine_barrier()
    for b_ in (rib, rsw, rr1, rr2, rkb):
        AR.free(b_)
    dump("kvT", kvTA[:, :, :].rearrange("p t d -> p (t d)"), [128, NT * 128], BF16)

    pbuf = AR.alloc([128, 2, 1024], BF16, hi=True)
    kstg = AR.alloc([128, 4, 2, 128], BF16, hi=True)
    vstg = AR.alloc([128, 4, 2, 128], BF16, hi=True)
    lrow0 = AR.alloc([1, 1024], F32, hi=True)
    obuf_a = AR.alloc([128, 2048], BF16, hi=True)
    obuf = obuf_a
    for t in range(S):
        Et = C * (t + 1)
        for g2 in range(2):
            emit_attn_pass("attn", t, g2, 0, Et, Et, qaT, None)
            emit_dance(t, g2, True, OscA)
    nc.all_engine_barrier()
    for b_ in (kv_rawA, kvTA, qaT, masksA, obuf_a, pbuf, kstg, vstg, lrow0):
        AR.free(b_)
    dump("OscA", OscA[:, :, :, :].rearrange("p s h q -> p (s h q)"), [128, S * 16 * 128], BF16)

    ACC = AR.alloc([128, S, D], F32)
    ACC_v = ACC
    WpH = AR.alloc([128, 16, 1024], BF16, hi=True)
    brow_proj = AR.alloc([1, D], BF16, hi=True)
    g("s_cg").inc(GP.dma_start(out=brow_proj[0:1, :], in_=b_proj_d.unsqueeze(0)), 16)
    for hh in range(2):
        g("s_cg").inc(GP.dma_start(out=WpH[:, :, :],
                              in_=W_proj_d[:, 1024 * hh:1024 * (hh + 1)].rearrange("(c p) f -> p c f", p=128)), 16)
        s_c0.wait(TE)
        g("s_cg").wait(TE)
        emit_poproj("out" + str(hh), OscA, WpH, x1_a, ACC, xb, junkL)
        nc.all_engine_barrier()
    AR.free(WpH)
    AR.free(OscA)
    AR.free(brow_proj)
    dump("ACC1", ACC[:, :, :].rearrange("p s d -> p (s d)"), [128, S * D], F32)

    tT = AR.alloc([128, S, 16, 128], BF16)
    ln2bc = AR.alloc([128, D], F32, hi=True)
    s_c0.inc(SY.dma_start(out=ln2bc[:, :], in_=ln2_d.unsqueeze(0).partition_broadcast(128)), 16)
    g("s_cg").inc(GP.dma_start(out=rw_sb[:, :, :], in_=router_w_d.rearrange("(c p) f -> p c f", p=128)), 16)
    s_c0.wait(V)
    g("s_cg").wait(V); s_c0.wait(TE); s_c0.wait(SC)
    g("s_cg").wait(TE)
    emit_p5pre(x1_a, tT, ln2bc, junkL, ot_stage)
    nc.all_engine_barrier()
    AR.free(x1_a)
    AR.free(ln2bc)
    dump("comb", comb[:, :, :].rearrange("p s e -> p (s e)"), [128, S * 8], F32)
    dump("tT", tT[:, :, :, :].rearrange("p s c q -> p (s c q)"), [128, S * 16 * 128], BF16)

    AR.free(xb)
    wgh_a = AR.alloc([128, 2, 16, 512], BF16, hi=True)
    wuh_a = AR.alloc([128, 2, 16, 512], BF16, hi=True)
    wdh_a = AR.alloc([128, 4, 2048], BF16, hi=True)
    p5tT2 = AR.alloc([128, 16, 128], BF16)
    emit_p5(tT, ACC, (wgh_a, wuh_a, wdh_a), ot_stage[:, 0:512], otT_scr, (xnT_scr, p5tT2),
            junkL[:, 0:1024])
    nc.all_engine_barrier()
    for b_ in (wgh_a, wuh_a, wdh_a, tT, junkL, p5tT2):
        AR.free(b_)

    xb = AR.alloc([128, 2, D], F32)
    out_scr = AR.alloc([128, 2, D], F32, hi=True)
    emit_p5f(ACC, xb, out_scr)
    SY.wait_ge(g("s_dbg").h, g("s_dbg").n)
    es.close()
    return nc, dbg_aps


def _inputs_for_core(cfg, core, consts, W):
    own = own_row_index(cfg, core)
    d = dict(consts)
    d["x_full"] = W["x"]
    d["x_own"] = np.ascontiguousarray(W["x"][own])
    d.update({k: W[k] for k in ("W_latent b_latent W_d b_d W_proj b_proj ln1_w ln2_w "
                                "wq wk wv wo qnorm_w knorm_w router_w router_bias "
                                "wg wu wd").split()})
    return d


def kernel(**inputs):
    cfg = Cfg(NT=64)
    x = np.ascontiguousarray(np.asarray(inputs["L0_attn_x"], np.float32).reshape(cfg.T, D))
    W = dict(
        x=x,
        W_latent=np.asarray(inputs["L0_attn_W_latent"], np.float32),
        b_latent=np.asarray(inputs["L0_attn_b_latent"], np.float32),
        W_d=np.asarray(inputs["L0_attn_W_d"], np.float32),
        b_d=np.asarray(inputs["L0_attn_b_d"], np.float32),
        W_proj=np.asarray(inputs["L0_attn_W_proj"], np.float32),
        b_proj=np.asarray(inputs["L0_attn_b_proj"], np.float32),
        ln1_w=np.asarray(inputs["L0_ffn_ln1_w"], np.float32),
        ln2_w=np.asarray(inputs["L0_ffn_ln2_w"], np.float32),
        wq=np.asarray(inputs["L0_ffn_wq"], np.float32),
        wk=np.asarray(inputs["L0_ffn_wk"], np.float32),
        wv=np.asarray(inputs["L0_ffn_wv"], np.float32),
        wo=np.asarray(inputs["L0_ffn_wo"], np.float32),
        qnorm_w=np.asarray(inputs["L0_ffn_qnorm_w"], np.float32),
        knorm_w=np.asarray(inputs["L0_ffn_knorm_w"], np.float32),
        router_w=np.asarray(inputs["L0_ffn_router_w"], np.float32),
        router_bias=np.asarray(inputs["L0_ffn_router_bias"], np.float32),
        wg=np.asarray(inputs["L0_ffn_wg"], np.float32),
        wu=np.asarray(inputs["L0_ffn_wu"], np.float32),
        wd=np.asarray(inputs["L0_ffn_wd"], np.float32),
    )
    nc, _ = build_program(cfg)
    in_maps = [_inputs_for_core(cfg, i, host_consts(cfg, i), W) for i in range(C)]
    res = run_bass_kernel_spmd(nc, in_maps, core_ids=list(range(C)))
    out = np.empty((cfg.T, D), np.float32)
    for i in range(C):
        out[own_row_index(cfg, i)] = res.results[i]["out_own"]
    return out.reshape(1, cfg.T, D)
